# version 09
# baseline (speedup 1.0000x reference)
import contextlib
import numpy as np
import concourse.bass as bass
import concourse.mybir as mybir
from concourse.bass_utils import run_bass_kernel_spmd
from concourse.aot_env import allow_ici_all_to_all

F32 = mybir.dt.float32
BF16 = mybir.dt.bfloat16
AF = mybir.ActivationFunctionType
ALU = mybir.AluOpType

Bsz, L, Dm = 2, 4096, 4096
CPROJ = 2576
NCC = 21
TT, NTT = 1024, 4
Q, NCH = 128, 32
EPS = 1e-5
INTER = 11008
NCI = 86


class KB:
    ENGS = ("sync", "gpsimd", "vector", "scalar", "tensor")

    def __init__(self, nc, ctx):
        self.nc = nc
        self.ctx = ctx
        self.pend = {e: [] for e in self.ENGS}
        self.sem = {}
        for name in ("vs", "ss", "ts", "co"):
            self.sem[name] = ctx.enter_context(nc.semaphore(name))
        self.cnt = {k: 0 for k in self.sem}

    def _get_sem(self, name):
        if name not in self.sem:
            self.sem[name] = self.ctx.enter_context(self.nc.semaphore(name))
            self.cnt[name] = 0
        return self.sem[name]

    def _mkwait(self, eng, waits):
        for wt in waits:
            if wt is None:
                continue
            s, v = wt
            if v and v > 0:
                eng.wait_ge(self.sem[s], v)

    def _serial(self, engname, key, make, waits):
        self.cnt[key] += 1
        tick = self.cnt[key]

        def fn(eng, make=make, waits=tuple(waits), tick=tick, key=key):
            self._mkwait(eng, waits)
            if tick > 1:
                eng.wait_ge(self.sem[key], tick - 1)
            make(eng).then_inc(self.sem[key], 1)

        self.pend[engname].append(fn)
        return (key, tick)

    def v(self, make, waits=()):
        return self._serial("vector", "vs", make, waits)

    def s(self, make, waits=()):
        return self._serial("scalar", "ss", make, waits)

    def t(self, make, waits=(), inc=False):
        tick = None
        if inc:
            self.cnt["ts"] += 1
            tick = self.cnt["ts"]

        def fn(eng, make=make, waits=tuple(waits), tick=tick):
            self._mkwait(eng, waits)
            ins = make(eng)
            if tick is not None:
                ins.then_inc(self.sem["ts"], 1)

        self.pend["tensor"].append(fn)
        return ("ts", tick) if inc else None

    def tmark(self):
        return self.t(lambda e: e.nop(), inc=True)

    def dma(self, q, make, waits=(), stream="d"):
        self._get_sem(stream)
        self.cnt[stream] += 16
        tick = self.cnt[stream]

        def fn(eng, make=make, waits=tuple(waits), key=stream):
            self._mkwait(eng, waits)
            make(eng).then_inc(self.sem[key], 16)

        self.pend[q].append(fn)
        return (stream, tick)

    def epoch(self, stream):
        self._get_sem(stream)
        return (stream, self.cnt[stream])

    def raw(self, engname, fn):
        self.pend[engname].append(fn)

    def vtick(self):
        return ("vs", self.cnt["vs"])

    def boundary(self):
        return [("vs", self.cnt["vs"]), ("ss", self.cnt["ss"]), ("ts", self.cnt["ts"])]

    def stick(self):
        return ("ss", self.cnt["ss"])

    def flush(self, block, scope=None):
        for name in self.ENGS:
            fns = self.pend[name]
            if not fns:
                continue
            self.pend[name] = []
            if scope is not None:
                def run(eng, fns=fns, scope=scope):
                    with self.nc.named_scope(scope):
                        for f in fns:
                            f(eng)
                getattr(block, name)(run)
            else:
                getattr(block, name)(lambda eng, fns=fns: [f(eng) for f in fns])


def build_program(debug=False, stage="full"):
    nc = bass.Bass("TRN2", target_bir_lowering=False)
    dt_io = "ExternalOutput" if debug else "Internal"

    def din(name, shape, dt=F32):
        return nc.dram_tensor(name, shape, dt, kind="ExternalInput").ap()

    x_fm = din("x_fm", [Dm, L])
    xq_fm = din("xq_fm", [Dm, 1024])
    w_in_t = din("w_in_t", [20, 128, 4096])
    w_in_dt = din("w_in_dt", [128, 512])
    conv_w_in = din("conv_w", [128, 48])
    conv_b_in = din("conv_b", [128, 12])
    dtb_in = din("dtb", [16, 1])
    alog_in = din("alog", [16, 1])
    drow_in = din("drow", [1, 1024])
    nwrow_in = din("nwrow", [1, 1024])
    w_out_t = din("w_out_t", [32, 128, 4096])
    w_gate_t = din("w_gate_t", [86, 128, 4096])
    w_up_t = din("w_up_t", [86, 128, 4096])
    w_down_t = din("w_down_t", [32, 128, 11008])
    gscol_in = din("gscol", [128, 1])
    uscol_in = din("uscol", [128, 1])
    dscol_in = din("dscol", [128, 1])
    bm0_in = din("bm0", [128, 1])
    bm1_in = din("bm1", [128, 1])
    triu_in = din("triu", [128, 128])
    ident_in = din("ident", [128, 128])

    out_fm = nc.dram_tensor("out_fm", [Dm, 1024], F32, kind="ExternalOutput").ap()

    d_gate = nc.dram_tensor("d_gate", [1024, L], BF16, kind=dt_io).ap()
    d_xbc = nc.dram_tensor("d_xbc", [1536, L], BF16, kind=dt_io).ap()
    d_s = nc.dram_tensor("d_s", [16, L], F32, kind=dt_io).ap()
    d_dt = nc.dram_tensor("d_dt", [16, L], F32, kind=dt_io).ap()
    d_sneg = nc.dram_tensor("d_sneg", [16, L], F32, kind="Internal").ap()
    d_E = nc.dram_tensor("d_E", [16, 32], F32, kind="Internal").ap()
    gin = nc.dram_tensor("gin", [2 * L, 1024], BF16, kind="Internal")
    dbg_gin = nc.dram_tensor("dbg_gin", [L, 1024], BF16, kind="ExternalOutput").ap() if debug else None
    gout = nc.dram_tensor("gout", [2 * L, 1024], BF16, kind="Internal")
    d_x2 = nc.dram_tensor("d_x2", [Dm, 1024], F32, kind=dt_io).ap()
    d_inter = nc.dram_tensor("d_inter", [INTER, 1024], BF16, kind=dt_io).ap()

    ctx = contextlib.ExitStack()
    with ctx:
        kb = KB(nc, ctx)
        blk = ctx.enter_context(nc.Block())
        SB = lambda name, shape, dt: ctx.enter_context(nc.sbuf_tensor(name, shape, dt))

        ident_f = SB("ident_f", [128, 128], F32)
        triu_f = SB("triu_f", [128, 128], F32)
        ones_row = SB("ones_row", [1, 128], F32)
        ones_colbf = SB("ones_colbf", [128, 1], BF16)
        a_col = SB("a_col", [16, 1], F32)
        dtb_col = SB("dtb_col", [16, 1], F32)
        gscol = SB("sb_gscol", [128, 1], F32)
        uscol = SB("sb_uscol", [128, 1], F32)
        dscol = SB("sb_dscol", [128, 1], F32)
        bm0 = SB("sb_bm0", [128, 1], F32)
        bm1 = SB("sb_bm1", [128, 1], F32)
        kcol = SB("kcol", [128, 2], F32)
        cw_sb = SB("cw_sb", [128, 48], F32)
        cb_sb = SB("cb_sb", [128, 12], F32)

        kb.dma("sync", lambda e: e.dma_start(out=ident_f[:], in_=ident_in), stream="cst")
        kb.dma("sync", lambda e: e.dma_start(out=triu_f[:], in_=triu_in), stream="cst")
        kb.dma("sync", lambda e: e.dma_start(out=dtb_col[:], in_=dtb_in), stream="cst")
        kb.dma("sync", lambda e: e.dma_start(out=gscol[:], in_=gscol_in), stream="cst")
        kb.dma("sync", lambda e: e.dma_start(out=uscol[:], in_=uscol_in), stream="cst")
        kb.dma("sync", lambda e: e.dma_start(out=dscol[:], in_=dscol_in), stream="cst")
        kb.dma("sync", lambda e: e.dma_start(out=bm0[:], in_=bm0_in), stream="cst")
        kb.dma("sync", lambda e: e.dma_start(out=bm1[:], in_=bm1_in), stream="cst")
        kb.dma("sync", lambda e: e.dma_start(out=a_col[:], in_=alog_in), stream="cst")
        kb.dma("sync", lambda e: e.dma_start(out=cw_sb[:], in_=conv_w_in), stream="cst")
        kb.dma("sync", lambda e: e.dma_start(out=cb_sb[:], in_=conv_b_in), stream="cst")

        ld_id = ld_tr = ld_dtb = ld_gs = ld_us = ld_ds = ld_bm0 = ld_bm1 = \
            ld_alog = ld_cw = ld_cb = kb.epoch("cst")
        v_ones = kb.v(lambda e: e.memset(ones_row[:], 1.0))
        kb.v(lambda e: e.memset(kcol[:, 0:1], EPS))
        v_kcol = kb.v(lambda e: e.memset(kcol[:, 1:2], 1.0))
        v_onesb = kb.v(lambda e: e.memset(ones_colbf[:], 1.0))
        s_a1 = kb.s(lambda e: e.activation(a_col[:], a_col[:], AF.Exp), waits=[ld_alog])
        v_a2 = kb.v(lambda e: e.tensor_scalar_mul(a_col[:], a_col[:], -1.0), waits=[s_a1])

        with contextlib.ExitStack() as p1:
            SB1 = lambda name, shape, dt: p1.enter_context(nc.sbuf_tensor(name, shape, dt))
            PS1 = lambda name, shape, dt: p1.enter_context(nc.psum_tensor(name, shape, dt))
            stg = SB1("stg", [128, 12 * 515], BF16)
            x_bf = [SB1(f"x_bf{i}", [128, 32 * 512], BF16) for i in range(2)]
            x2sq = [SB1(f"x2sq{i}", [128, 512], BF16) for i in range(2)]
            slab = [SB1(f"slab{i}", [128, 4096], BF16) for i in range(3)]
            r1rep = SB1("r1rep", [128, 512], F32)
            m_row = SB1("m_row", [1, 512], F32)
            var_row = SB1("var_row", [1, 512], F32)
            r_row = SB1("r_row", [1, 512], F32)
            mneg_row = SB1("mneg_row", [1, 512], F32)
            rsum_rows = SB1("rsum_rows", [1, 21 * 128], F32)
            rsc_sb = SB1("rsc_sb", [128, 1], F32)
            gate_tmp = [SB1(f"gate_tmp{i}", [128, 512], F32) for i in range(2)]
            sgate_bf = [SB1(f"sgate_bf{i}", [128, 512], BF16) for i in range(2)]
            cv_tmp = [SB1(f"cv_tmp{i}", [128, 512], F32) for i in range(2)]
            dtt_tmp = [SB1(f"dtt_tmp{i}", [16, 512], F32) for i in range(2)]
            xbc_tmp = [SB1(f"xbc_tmp{i}", [128, 512], BF16) for i in range(2)]
            ps_m2 = [PS1(f"ps_m{i}", [128, 512], F32) for i in range(2)]
            ps_rcol = PS1("ps_rcol", [128, 1], F32)
            ps_rrow = PS1("ps_rrow", [1, 128], F32)

            for cc in range(12):
                kb.v(lambda e, cc=cc: e.memset(stg[:, cc * 515:cc * 515 + 3], 0.0))

            sq_mm = [None] * 32
            slab_last = [None, None, None]
            x_consumed = [None, None]
            stat_ev_prev = None
            main_free = [None, None]
            rs_ev_prev = None
            gate_st = [None, None]
            gate_sil = [None, None]
            xbc_st = [None, None]
            cv_free = [None, None]
            dtt_st = [None, None]

            NT8 = 8
            xstg = [SB1(f"xstg{i}", [128, 512], F32) for i in range(4)]
            cast_done = [None] * 32
            stat_ready = [None] * NT8

            def issue_xloads(tt, xc_wait):
                t0 = tt * 512
                xb = x_bf[tt % 2]
                for dc in range(32):
                    sg = xstg[dc % 4]
                    if dc >= 4:
                        w = [cast_done[dc - 4]]
                    else:
                        w = [xc_wait] if xc_wait else []
                        if cast_done[28 + dc]:
                            w.append(cast_done[28 + dc])
                    lt = kb.dma("sync", lambda e, dc=dc, t0=t0, sg=sg: e.dma_start(
                        out=sg[:], in_=x_fm[dc * 128:(dc + 1) * 128, t0:t0 + 512]),
                        waits=w, stream=f"xs{dc % 4}")
                    w2 = [lt]
                    if xc_wait and dc < 4:
                        w2.append(xc_wait)
                    cast_done[dc] = kb.s(lambda e, dc=dc, sg=sg, xb=xb: e.copy(
                        xb[:, dc * 512:(dc + 1) * 512], sg[:]), waits=w2)

            def emit_stats(tt):
                w_ps = [m for m in main_free if m]
                for dc in range(32):
                    buf = x2sq[dc % 2]
                    w = [cast_done[dc], v_onesb]
                    if dc >= 2:
                        w.append(sq_mm[dc - 2])
                    elif sq_mm[30 + dc]:
                        w.append(sq_mm[30 + dc])
                    sq = kb.s(
                        lambda e, dc=dc, buf=buf, tt=tt: e.activation(
                            buf[:], x_bf[tt % 2][:, dc * 512:(dc + 1) * 512], AF.Square),
                        waits=w)
                    w2 = [cast_done[dc]]
                    if dc == 0:
                        w2 += [v_onesb] + w_ps
                    kb.t(lambda e, dc=dc, tt=tt: e.matmul(
                        ps_m2[0][0:1, 0:512],
                        ones_colbf[:],
                        x_bf[tt % 2][:, dc * 512:(dc + 1) * 512],
                        start=(dc == 0), stop=(dc == 31)),
                        waits=w2)
                    sq_mm[dc] = kb.t(lambda e, dc=dc, buf=buf: e.matmul(
                        ps_m2[1][0:1, 0:512],
                        ones_colbf[:],
                        buf[:],
                        start=(dc == 0), stop=(dc == 31)),
                        waits=[sq] + (w2[1:] if dc == 0 else []), inc=True)
                kb.v(lambda e: e.tensor_scalar(m_row[:], ps_m2[0][0:1, 0:512], 1.0 / Dm, None, ALU.mult),
                     waits=[sq_mm[31]])
                kb.v(lambda e: e.tensor_scalar(var_row[:], ps_m2[1][0:1, 0:512], 1.0 / Dm, None, ALU.mult))
                kb.v(lambda e: e.tensor_mul(r_row[:], m_row[:], m_row[:]))
                v_var = kb.v(lambda e: e.tensor_sub(var_row[:], var_row[:], r_row[:]))
                kb.s(lambda e: e.activation(r_row[:], var_row[:], AF.Ln, bias=kcol[0:1, 0:1]),
                     waits=[v_var, v_kcol])
                s_r = kb.s(lambda e: e.activation(r_row[:], r_row[:], AF.Exp, scale=-0.5))
                v_mn = kb.v(lambda e: e.tensor_scalar_mul(mneg_row[:], m_row[:], -1.0))
                rep_mm = kb.t(lambda e: e.matmul(
                    ps_m2[0][:, 0:512], ones_row[:], r_row[0:1, 0:512],
                    start=True, stop=True),
                    waits=[s_r, v_ones], inc=True)
                rep_ev = kb.v(lambda e: e.tensor_copy(r1rep[:], ps_m2[0][:]), waits=[rep_mm])
                main_free[0] = rep_ev
                main_free[1] = rep_ev
                return (rep_ev, v_mn)

            issue_xloads(0, None)
            stat_ready[0] = emit_stats(0)
            for tt in range(NT8):
                t0 = tt * 512
                xb = x_bf[tt % 2]
                if tt + 1 < NT8:
                    issue_xloads(tt + 1, x_consumed[(tt + 1) % 2])
                rep_ev, v_mn = stat_ready[tt]

                for c in range(NCC):
                    cs = 128 if c < 20 else 16
                    psm = ps_m2[c % 2]
                    sli = (tt * NCC + c) % 3
                    sb = slab[sli]
                    w = [slab_last[sli]] if slab_last[sli] else []
                    if c < 20:
                        sl = kb.dma("gpsimd",
                                    lambda e, c=c, sb=sb: e.dma_start(out=sb[:], in_=w_in_t[c]),
                                    waits=w, stream=f"slb{sli}")
                    else:
                        sl = kb.dma("gpsimd",
                                    lambda e, sb=sb: e.dma_start(out=sb[:, 0:512], in_=w_in_dt),
                                    waits=w, stream=f"slb{sli}")
                    rs_row_ev = None
                    if tt == 0:
                        rs_done = None
                        for k in range(32):
                            w2 = []
                            if k == 0:
                                w2 = [sl, v_onesb]
                                if rs_ev_prev:
                                    w2.append(rs_ev_prev)
                            rs_done = kb.t(lambda e, k=k, sb=sb, cs=cs: e.matmul(
                                ps_rcol[0:cs, 0:1],
                                sb[:, k * cs:(k + 1) * cs],
                                ones_colbf[:],
                                start=(k == 0), stop=(k == 31)),
                                waits=w2, inc=(k == 31))
                        rs_ev = kb.v(lambda e, cs=cs: e.tensor_copy(rsc_sb[0:cs, 0:1], ps_rcol[0:cs, 0:1]),
                                     waits=[rs_done])
                        rs_ev_prev = rs_ev
                        tp = kb.t(lambda e, cs=cs: e.transpose(
                            ps_rrow[0:1, 0:cs], rsc_sb[0:cs, 0:1], ident_f[0:cs, 0:cs]),
                            waits=[rs_ev, ld_id], inc=True)
                        rs_row_ev = kb.v(lambda e, c=c, cs=cs: e.tensor_copy(
                            rsum_rows[0:1, c * 128:c * 128 + cs], ps_rrow[0:1, 0:cs]), waits=[tp])
                    for k in range(32):
                        w2 = []
                        if k == 0:
                            w2 = [sl]
                            if main_free[c % 2]:
                                w2.append(main_free[c % 2])
                        kb.t(lambda e, k=k, sb=sb, cs=cs, psm=psm, xb=xb: e.matmul(
                            psm[0:cs, 0:512],
                            sb[:, k * cs:(k + 1) * cs],
                            xb[:, k * 512:(k + 1) * 512],
                            start=(k == 0), stop=False),
                            waits=w2)
                    corr_w = [v_mn]
                    if rs_row_ev:
                        corr_w.append(rs_row_ev)
                    mdone = kb.t(lambda e, c=c, cs=cs, psm=psm: e.matmul(
                        psm[0:cs, 0:512],
                        rsum_rows[0:1, c * 128:c * 128 + cs],
                        mneg_row[0:1, 0:512],
                        start=False, stop=True),
                        waits=corr_w, inc=True)
                    slab_last[sli] = mdone
                    if c < 8:
                        gb, sgb = gate_tmp[c % 2], sgate_bf[c % 2]
                        w2 = [mdone, rep_ev]
                        if gate_sil[c % 2]:
                            w2.append(gate_sil[c % 2])
                        gv = kb.v(lambda e, gb=gb, psm=psm: e.tensor_mul(
                            gb[:], psm[0:128, 0:512], r1rep[:]), waits=w2)
                        w3 = [gv]
                        if gate_st[c % 2]:
                            w3.append(gate_st[c % 2])
                        gs = kb.s(lambda e, gb=gb, sgb=sgb: e.activation(sgb[:], gb[:], AF.Silu),
                                  waits=w3)
                        gate_sil[c % 2] = gs
                        gate_st[c % 2] = kb.dma("sync",
                                                lambda e, c=c, t0=t0, sgb=sgb: e.dma_start(
                                                    out=d_gate[c * 128:(c + 1) * 128, t0:t0 + 512],
                                                    in_=sgb[:]),
                                                waits=[gs], stream=f"gst{c % 2}")
                        main_free[c % 2] = gv
                    elif c < 20:
                        cc = c - 8
                        main_free[c % 2] = kb.v(lambda e, cc=cc, psm=psm: e.tensor_mul(
                            stg[:, cc * 515 + 3:cc * 515 + 515],
                            psm[0:128, 0:512], r1rep[:]),
                            waits=[mdone, rep_ev])
                    else:
                        w4 = [mdone, rep_ev]
                        if dtt_st[tt % 2]:
                            w4.append(dtt_st[tt % 2])
                        dtb_buf = dtt_tmp[tt % 2]
                        dv = kb.v(lambda e, psm=psm, dtb_buf=dtb_buf: e.tensor_mul(
                            dtb_buf[:],
                            psm[0:16, 0:512], r1rep[0:16, :]),
                            waits=w4)
                        dtt_st[tt % 2] = kb.dma("sync", lambda e, t0=t0, dtb_buf=dtb_buf: e.dma_start(
                            out=d_dt[0:16, t0:t0 + 512],
                            in_=dtb_buf[:]), waits=[dv], stream=f"dts{tt % 2}")
                        main_free[c % 2] = dv
                x_consumed[tt % 2] = mdone
                if tt + 1 < NT8:
                    stat_ready[tt + 1] = emit_stats(tt + 1)
                for cc in range(12):
                    base = cc * 515
                    cb = cv_tmp[cc % 2]
                    xbc = xbc_tmp[cc % 2]
                    w = [ld_cw, ld_cb]
                    if cv_free[cc % 2]:
                        w.append(cv_free[cc % 2])
                    kb.v(lambda e, base=base, cb=cb, cc=cc: e.tensor_scalar(
                        cb[:], stg[:, base:base + 512],
                        cw_sb[:, cc * 4:cc * 4 + 1], cb_sb[:, cc:cc + 1],
                        ALU.mult, ALU.add), waits=w)
                    for k in range(1, 4):
                        kb.v(lambda e, base=base, cb=cb, cc=cc, k=k: e.scalar_tensor_tensor(
                            cb[:], stg[:, base + k:base + k + 512],
                            cw_sb[:, cc * 4 + k:cc * 4 + k + 1], cb[:],
                            ALU.mult, ALU.add))
                    cv_v = kb.vtick()
                    w3 = [cv_v]
                    if xbc_st[cc % 2]:
                        w3.append(xbc_st[cc % 2])
                    cv_free[cc % 2] = kb.s(
                        lambda e, cb=cb, xbc=xbc: e.activation(xbc[:], cb[:], AF.Silu),
                        waits=w3)
                    xbc_st[cc % 2] = kb.dma("sync",
                                            lambda e, cc=cc, t0=t0, xbc=xbc: e.dma_start(
                                                out=d_xbc[cc * 128:(cc + 1) * 128, t0:t0 + 512],
                                                in_=xbc[:]),
                                            waits=[cv_free[cc % 2]], stream=f"xst{cc % 2}")
                    kb.v(lambda e, base=base: e.tensor_copy(
                        stg[:, base:base + 3], stg[:, base + 512:base + 515]))

            if stage in ("p1", "tt1"):
                eps1 = [kb.epoch(s) for s in ("gst0", "gst1", "xst0", "xst1", "dts0", "dts1")]

                def fin1(e, eps1=eps1):
                    for s, v in eps1:
                        e.wait_ge(kb.sem[s], v)
                kb.raw("sync", fin1)
            kb.flush(blk, scope="p1")
            spill_eps = [kb.epoch("gst0"), kb.epoch("gst1"),
                         kb.epoch("xst0"), kb.epoch("xst1")]
            dt_eps = [kb.epoch("dts0"), kb.epoch("dts1")]
            bnd_p1 = kb.boundary() + spill_eps + dt_eps
        if stage in ("p1", "tt1"):
            return nc

        with contextlib.ExitStack() as p2:
            SB2 = lambda name, shape, dt: p2.enter_context(nc.sbuf_tensor(name, shape, dt))
            PS2 = lambda name, shape, dt: p2.enter_context(nc.psum_tensor(name, shape, dt))
            dtp = SB2("dtp", [16, L], F32)
            eE = SB2("eE", [16, L], F32)
            wW = SB2("wW", [16, L], F32)
            drep = SB2("drep", [128, 1024], F32)
            nwrep = SB2("nwrep", [128, 1024], F32)
            Erepcols = SB2("Erepcols", [128, 512], F32)
            Etile = SB2("Etile", [16, 32], F32)
            Erow = SB2("Erow", [1, 512], F32)
            drow_sb = SB2("drow_sb", [1, 1024], F32)
            nwrow_sb = SB2("nwrow_sb", [1, 1024], F32)

            with contextlib.ExitStack() as p1d:
                sS = p1d.enter_context(nc.sbuf_tensor("sS", [16, L], F32))
                tmp16 = p1d.enter_context(nc.sbuf_tensor("tmp16", [16, L], F32))
                ps_tmp = p1d.enter_context(nc.psum_tensor("ps_tmp", [128, 1024], F32))
                kb.dma("sync", lambda e: e.dma_start(out=drow_sb[:], in_=drow_in), waits=bnd_p1, stream="cst2")
                kb.dma("sync", lambda e: e.dma_start(out=nwrow_sb[:], in_=nwrow_in), stream="cst2")
                kb.dma("sync", lambda e: e.dma_start(out=dtp[:], in_=d_dt),
                       waits=dt_eps, stream="cst2")
                ld_dr = ld_nw = ld_dt = kb.epoch("cst2")
                kb.v(lambda e: e.tensor_scalar(dtp[:], dtp[:], dtb_col[:, 0:1], None, ALU.add),
                     waits=[ld_dtb, ld_dt])
                s_abs = kb.s(lambda e: e.activation(tmp16[:], dtp[:], AF.Abs),
                             waits=[kb.vtick()])
                kb.s(lambda e: e.activation(tmp16[:], tmp16[:], AF.Exp, scale=-1.0))
                sp2 = kb.s(lambda e: e.activation(tmp16[:], tmp16[:], AF.Ln, bias=kcol[0:16, 1:2]))
                kb.v(lambda e: e.tensor_scalar(dtp[:], dtp[:], 0.0, None, ALU.max),
                     waits=[s_abs])
                kb.v(lambda e: e.tensor_add(dtp[:], dtp[:], tmp16[:]), waits=[sp2])
                kb.v(lambda e: e.tensor_scalar(sS[:], dtp[:], a_col[:, 0:1], None, ALU.mult),
                     waits=[v_a2])
                for c in range(NCH):
                    kb.v(lambda e, c=c: e.tensor_tensor_scan(
                        sS[:, c * 128:(c + 1) * 128], sS[:, c * 128:(c + 1) * 128],
                        sS[:, c * 128:(c + 1) * 128], 0.0, ALU.add, ALU.bypass))
                scan_done = kb.vtick()
                se = kb.s(lambda e: e.activation(eE[:], sS[:], AF.Exp), waits=[scan_done])
                v_sneg = kb.v(lambda e: e.tensor_scalar_mul(tmp16[:], sS[:], -1.0))
                for c in range(NCH):
                    kb.s(lambda e, c=c: e.activation(
                        wW[:, c * 128:(c + 1) * 128], sS[:, c * 128:(c + 1) * 128],
                        AF.Exp, scale=-1.0,
                        bias=sS[:, c * 128 + 127:c * 128 + 128]),
                        waits=[scan_done] if c == 0 else [])
                w_exp = kb.stick()
                v_w = kb.v(lambda e: e.tensor_mul(wW[:], wW[:], dtp[:]), waits=[w_exp])
                v_E = kb.v(lambda e: e.tensor_copy(Etile[:], eE[:, 127:L:128]), waits=[se])
                kb.dma("sync", lambda e: e.dma_start(out=d_s, in_=sS[:]), waits=[scan_done], stream="bnc")
                kb.dma("sync", lambda e: e.dma_start(out=d_sneg, in_=tmp16[:]), waits=[v_sneg], stream="bnc")
                kb.dma("sync", lambda e: e.dma_start(out=d_E, in_=Etile[:]), waits=[v_E], stream="bnc")
                st_s = st_sn = st_E = kb.epoch("bnc")
                ld_E = kb.dma("sync",
                              lambda e: e.dma_start(
                                  out=Erow[0:1, :].rearrange("p (h c) -> p h c", c=32),
                                  in_=d_E),
                              waits=[st_E], stream="cst2")
                emm = kb.t(lambda e: e.matmul(ps_tmp[:, 0:512], ones_row[:], Erow[:],
                                              start=True, stop=True),
                           waits=[ld_E, v_ones], inc=True)
                erep_ev = kb.v(lambda e: e.tensor_copy(Erepcols[:], ps_tmp[:, 0:512]), waits=[emm])
                dmm = kb.t(lambda e: e.matmul(ps_tmp[:, 0:512], ones_row[:], drow_sb[0:1, 0:512],
                                              start=True, stop=True), waits=[ld_dr, erep_ev])
                dmm = kb.t(lambda e: e.matmul(ps_tmp[:, 512:1024], ones_row[:], drow_sb[0:1, 512:1024],
                                              start=True, stop=True), inc=True)
                drep_ev = kb.v(lambda e: e.tensor_copy(drep[:], ps_tmp[:]), waits=[dmm])
                nmm = kb.t(lambda e: e.matmul(ps_tmp[:, 0:512], ones_row[:], nwrow_sb[0:1, 0:512],
                                              start=True, stop=True),
                           waits=[ld_nw, drep_ev])
                nmm = kb.t(lambda e: e.matmul(ps_tmp[:, 512:1024], ones_row[:], nwrow_sb[0:1, 512:1024],
                                              start=True, stop=True), inc=True)
                nwrep_ev = kb.v(lambda e: e.tensor_copy(nwrep[:], ps_tmp[:]), waits=[nmm])
                kb.flush(blk, scope="p1d")
                bnd_p1d = kb.boundary() + [kb.epoch("bnc"), kb.epoch("cst2")]

            S_sb = SB2("S_sb", [128, 1024], F32)
            S_tmp = SB2("S_tmp", [128, 1024], F32)
            S_bf = SB2("S_bf", [128, 1024], BF16)
            x_tm = [SB2(f"x_tm{i}", [128, 1024], BF16) for i in range(2)]
            sgT = [SB2(f"sgT{i}", [128, 1024], BF16) for i in range(2)]
            B_fm = [SB2(f"B_fm{i}", [128, 256], BF16) for i in range(2)]
            C_fm = [SB2(f"C_fm{i}", [128, 256], BF16) for i in range(2)]
            B_tm = [SB2(f"B_tm{i}", [128, 256], BF16) for i in range(2)]
            srow_t = [SB2(f"srow_t{i}", [2, 2048], F32) for i in range(2)]
            negs_t = [SB2(f"negs_t{i}", [2, 2048], F32) for i in range(2)]
            tcols = SB2("tcols", [128, 48], F32)
            dec_min = SB2("dec_min", [128, 512], F32)
            decT = SB2("decT", [128, 2048], BF16)
            scoreTm = SB2("scoreTm", [128, 256], BF16)
            MT = SB2("MT", [128, 2048], BF16)
            xt_w = SB2("xt_w", [128, 1024], BF16)
            YsbT = SB2("YsbT", [128, 1024], F32)
            dxt = SB2("dxt", [128, 1024], F32)
            Ysq = SB2("Ysq", [128, 1024], BF16)
            ssq_col = SB2("ssq_col", [128, 2], F32)
            rstd_col = SB2("rstd_col", [128, 2], F32)
            yn_t = SB2("yn_t", [128, 1024], F32)
            yn_bf = SB2("yn_bf", [128, 1024], BF16)
            ps_r = PS2("ps_r", [128, 512], F32)
            ps_aux = PS2("ps_aux", [128, 512], F32)
            ps_y = PS2("ps_y", [128, 1024], F32)
            ps_yi = PS2("ps_yi", [128, 1024], F32)
            ps_sloc = PS2("ps_sloc", [128, 1024], F32)

            for i in range(2):
                kb.v(lambda e, i=i: e.memset(srow_t[i][:], 1.0))
                kb.v(lambda e, i=i: e.memset(negs_t[i][:], 1.0))
            kb.v(lambda e: e.memset(S_sb[:], 0.0))
            rows_ready = kb.vtick()

            loads = [None] * NCH
            chunk_done = [None] * NCH

            def issue_loads(c):
                i = c % 2
                w = list(chunk_done[c - 2]) if c >= 2 and chunk_done[c - 2] else []
                st = f"sL{i}"
                kb.dma("sync", lambda e, c=c, i=i: e.dma_start(
                    out=x_tm[i][:], in_=d_xbc[0:1024, c * 128:(c + 1) * 128], transpose=True),
                    waits=w + spill_eps + (bnd_p1d if c < 2 else []), stream=st)
                kb.dma("sync", lambda e, c=c, i=i: e.dma_start(
                    out=sgT[i][:], in_=d_gate[0:1024, c * 128:(c + 1) * 128], transpose=True),
                    stream=st)
                for gl in range(2):
                    kb.dma("sync", lambda e, c=c, i=i, gl=gl: e.dma_start(
                        out=B_fm[i][:, gl * 128:(gl + 1) * 128],
                        in_=d_xbc[1024 + gl * 128:1024 + (gl + 1) * 128, c * 128:(c + 1) * 128]),
                        stream=st)
                    kb.dma("sync", lambda e, c=c, i=i, gl=gl: e.dma_start(
                        out=C_fm[i][:, gl * 128:(gl + 1) * 128],
                        in_=d_xbc[1280 + gl * 128:1280 + (gl + 1) * 128, c * 128:(c + 1) * 128]),
                        stream=st)
                    kb.dma("sync", lambda e, c=c, i=i, gl=gl: e.dma_start(
                        out=B_tm[i][:, gl * 128:(gl + 1) * 128],
                        in_=d_xbc[1024 + gl * 128:1024 + (gl + 1) * 128, c * 128:(c + 1) * 128],
                        transpose=True),
                        stream=st)
                kb.dma("sync", lambda e, c=c, i=i: e.dma_start(
                    out=srow_t[i][0:1, :].rearrange("p (h t) -> p h t", t=128),
                    in_=d_s[:, c * 128:(c + 1) * 128]),
                    waits=[st_s, rows_ready], stream=st)
                kb.dma("sync", lambda e, c=c, i=i: e.dma_start(
                    out=negs_t[i][1:2, :].rearrange("p (h t) -> p h t", t=128),
                    in_=d_sneg[:, c * 128:(c + 1) * 128]),
                    waits=[st_sn], stream=st)
                loads[c] = kb.epoch(st)

            issue_loads(0)
            issue_loads(1)
            prev_ss_end = None
            aux_free = nwrep_ev
            r_free = nwrep_ev
            y_free = nwrep_ev

            def bc(src_ap, nh, rep, hstep):
                return bass.AP(tensor=src_ap.tensor, offset=src_ap.offset,
                               ap=[[src_ap.ap[0][0], 128], [hstep, nh], [0, rep]])

            for c in range(NCH):
                i = c % 2
                ld = loads[c]
                w0 = [ld_id]
                if aux_free:
                    w0.append(aux_free)
                tp1 = kb.t(lambda e, c=c: e.transpose(
                    ps_aux[0:128, 256:272], dtp[0:16, c * 128:(c + 1) * 128],
                    ident_f[0:16, 0:16]), waits=w0, inc=True)
                ev1 = kb.v(lambda e: e.tensor_copy(tcols[:, 0:16], ps_aux[0:128, 256:272]), waits=[tp1])
                tp2 = kb.t(lambda e, c=c: e.transpose(
                    ps_aux[0:128, 272:288], eE[0:16, c * 128:(c + 1) * 128],
                    ident_f[0:16, 0:16]), waits=[ev1], inc=True)
                ev2 = kb.v(lambda e: e.tensor_copy(tcols[:, 16:32], ps_aux[0:128, 272:288]), waits=[tp2])
                tp3 = kb.t(lambda e, c=c: e.transpose(
                    ps_aux[0:128, 288:304], wW[0:16, c * 128:(c + 1) * 128],
                    ident_f[0:16, 0:16]), waits=[ev2, v_w], inc=True)
                ev3 = kb.v(lambda e: e.tensor_copy(tcols[:, 32:48], ps_aux[0:128, 288:304]), waits=[tp3])

                exp_prev = None
                vmin = None
                for r4 in range(4):
                    rmm = None
                    for hh in range(4):
                        h = r4 * 4 + hh
                        w2 = []
                        if hh == 0:
                            w2 = [ld]
                            w2.append(r_free if r4 == 0 else vmin)
                        rmm = kb.t(lambda e, h=h, hh=hh, i=i: e.matmul(
                            ps_r[:, hh * 128:(hh + 1) * 128],
                            negs_t[i][0:2, h * 128:(h + 1) * 128],
                            srow_t[i][0:2, h * 128:(h + 1) * 128],
                            start=True, stop=True),
                            waits=w2, inc=(hh == 3))
                    w2 = [rmm]
                    if exp_prev:
                        w2.append(exp_prev)
                    vmin = kb.v(lambda e: e.tensor_scalar(dec_min[:], ps_r[:], 0.0, None, ALU.min),
                                waits=w2)
                    exp_prev = kb.s(lambda e, r4=r4: e.activation(
                        decT[:, r4 * 512:(r4 + 1) * 512], dec_min[:], AF.Exp), waits=[vmin])
                dec_done = exp_prev
                r_free = vmin

                smm = None
                for gl in range(2):
                    w2 = [ld]
                    if gl == 0:
                        w2.append(ev3)
                    smm = kb.t(lambda e, gl=gl, i=i: e.matmul(
                        ps_aux[:, gl * 128:(gl + 1) * 128],
                        B_fm[i][:, gl * 128:(gl + 1) * 128],
                        C_fm[i][:, gl * 128:(gl + 1) * 128],
                        start=True, stop=True),
                        waits=w2, inc=(gl == 1))
                for gl in range(2):
                    kb.v(lambda e, gl=gl: e.tensor_mul(
                        scoreTm[:, gl * 128:(gl + 1) * 128],
                        ps_aux[:, gl * 128:(gl + 1) * 128], triu_f[:]),
                        waits=[smm, ld_tr] if gl == 0 else [])
                aux_free = kb.vtick()
                kb.v(lambda e: e.tensor_mul(
                    MT[:].rearrange("p (h i) -> p h i", i=128),
                    decT[:].rearrange("p (h i) -> p h i", i=128),
                    bc(tcols[:, 0:16], 16, 128, 1)),
                     waits=[dec_done])
                score_rep = bass.AP(tensor=scoreTm[:].tensor, offset=scoreTm[:].offset,
                                    ap=[[scoreTm[:].ap[0][0], 128], [128, 2], [0, 8], [1, 128]])
                kb.v(lambda e, score_rep=score_rep: e.tensor_mul(
                    MT[:].rearrange("p (g h i) -> p g h i", g=2, i=128),
                    MT[:].rearrange("p (g h i) -> p g h i", g=2, i=128), score_rep))
                mt_done = kb.vtick()
                kb.v(lambda e, i=i: e.tensor_mul(
                    xt_w[:].rearrange("p (h j) -> p h j", j=64),
                    x_tm[i][:].rearrange("p (h j) -> p h j", j=64),
                    bc(tcols[:, 32:48], 16, 64, 1)),
                     waits=[ld])
                xtw_done = kb.vtick()

                if c > 0:
                    sbf = kb.v(lambda e: e.tensor_copy(S_bf[:], S_sb[:]))
                    yimm = None
                    for gl in range(2):
                        yimm = kb.t(lambda e, gl=gl, i=i: e.matmul(
                            ps_yi[:, gl * 512:(gl + 1) * 512],
                            C_fm[i][:, gl * 128:(gl + 1) * 128],
                            S_bf[:, gl * 512:(gl + 1) * 512],
                            start=True, stop=True),
                            waits=[sbf] if gl == 0 else [], inc=(gl == 1))
                    w2 = [yimm]
                    if prev_ss_end:
                        w2.append(prev_ss_end)
                    kb.v(lambda e: e.tensor_mul(
                        YsbT[:].rearrange("p (h j) -> p h j", j=64),
                        ps_yi[:].rearrange("p (h j) -> p h j", j=64),
                        bc(tcols[:, 16:32], 16, 64, 1)),
                         waits=w2)
                else:
                    w2 = [prev_ss_end] if prev_ss_end else []
                    kb.v(lambda e: e.memset(YsbT[:], 0.0), waits=w2)
                yi_done = kb.vtick()

                ymm = None
                for h in range(16):
                    gl = h // 8
                    off = gl * 512 + (h - gl * 8) * 64
                    w2 = []
                    if h == 0:
                        w2 = [mt_done, ld]
                        if y_free:
                            w2.append(y_free)
                    ymm = kb.t(lambda e, h=h, off=off, i=i: e.matmul(
                        ps_y[:, off:off + 64],
                        MT[:, h * 128:(h + 1) * 128],
                        x_tm[i][:, h * 64:(h + 1) * 64],
                        start=True, stop=True),
                        waits=w2, inc=(h == 15))
                slmm = None
                for gl in range(2):
                    w2 = [ld]
                    if gl == 0:
                        w2.append(xtw_done)
                    slmm = kb.t(lambda e, gl=gl, i=i: e.matmul(
                        ps_sloc[:, gl * 512:(gl + 1) * 512],
                        B_tm[i][:, gl * 128:(gl + 1) * 128],
                        xt_w[:, gl * 512:(gl + 1) * 512],
                        start=True, stop=True),
                        waits=w2, inc=(gl == 1))
                es = Erepcols[:, c:c + 1]
                erep_b = bass.AP(tensor=es.tensor, offset=es.offset,
                                 ap=[[es.ap[0][0], 128], [32, 16], [0, 64]])
                kb.v(lambda e, erep_b=erep_b: e.tensor_mul(
                    S_tmp[:].rearrange("p (h j) -> p h j", j=64),
                    S_sb[:].rearrange("p (h j) -> p h j", j=64), erep_b),
                     waits=[slmm, erep_ev])
                kb.v(lambda e: e.tensor_add(S_sb[:], S_tmp[:], ps_sloc[:]))
                s_upd = kb.vtick()
                kb.v(lambda e: e.tensor_add(YsbT[:], YsbT[:], ps_y[:]), waits=[ymm])
                y_free = kb.vtick()
                kb.v(lambda e, i=i: e.tensor_mul(dxt[:], x_tm[i][:], drep[:]),
                     waits=[drep_ev])
                kb.v(lambda e: e.tensor_add(YsbT[:], YsbT[:], dxt[:]))
                ysq = kb.s(lambda e: e.activation(Ysq[:], YsbT[:], AF.Square),
                           waits=[kb.vtick()])
                for gl in range(2):
                    kb.v(lambda e, gl=gl: e.reduce_sum(
                        ssq_col[:, gl:gl + 1], Ysq[:, gl * 512:(gl + 1) * 512],
                        axis=mybir.AxisListType.X),
                        waits=[ysq] if gl == 0 else [])
                kb.s(lambda e: e.activation(rstd_col[:], ssq_col[:], AF.Ln,
                                            bias=kcol[:, 0:1], scale=1.0 / 512),
                     waits=[kb.vtick()])
                rs2 = kb.s(lambda e: e.activation(rstd_col[:], rstd_col[:], AF.Exp, scale=-0.5))
                kb.v(lambda e: e.tensor_mul(
                    yn_t[:].rearrange("p (g j) -> p g j", j=512),
                    YsbT[:].rearrange("p (g j) -> p g j", j=512),
                    bc(rstd_col[:, 0:2], 2, 512, 1)),
                     waits=[rs2])
                kb.v(lambda e: e.tensor_mul(yn_t[:], yn_t[:], nwrep[:]), waits=[nwrep_ev])
                w2 = [ld, kb.epoch("yst")]
                kb.v(lambda e, i=i: e.tensor_mul(yn_bf[:], yn_t[:], sgT[i][:]), waits=w2)
                yn_v = kb.vtick()
                kb.dma("sync", lambda e, c=c: e.dma_start(
                    out=gin.ap()[c * 128:(c + 1) * 128, :], in_=yn_bf[:]), waits=[yn_v], stream="yst")
                kb.dma("sync", lambda e, c=c: e.dma_start(
                    out=gin.ap()[L + c * 128:L + (c + 1) * 128, :], in_=yn_bf[:]), waits=[yn_v], stream="yst")
                if debug:
                    kb.dma("sync", lambda e, c=c: e.dma_start(
                        out=dbg_gin[c * 128:(c + 1) * 128, :], in_=yn_bf[:]), waits=[yn_v], stream="yst")
                prev_ss_end = kb.stick()
                chunk_done[c] = [kb.vtick(), ("ts", kb.cnt["ts"])]
                if c + 2 < NCH:
                    issue_loads(c + 2)

            yn_all = kb.epoch("yst")

            def coll(eng):
                eng.wait_ge(kb.sem["yst"], yn_all[1])
                with allow_ici_all_to_all():
                    eng.collective_compute(
                        "AllToAll", ALU.bypass,
                        replica_groups=[[0, 1, 2, 3, 4, 5, 6, 7]],
                        ins=[gin.ap().opt()],
                        outs=[gout.ap().opt()],
                    ).then_inc(kb.sem["co"], 1)

            if stage != "ssd":
                kb.raw("gpsimd", coll)
            else:
                eps2 = [kb.epoch("yst")]

                def fin2(e, eps2=eps2):
                    for s, v in eps2:
                        e.wait_ge(kb.sem[s], v)
                kb.raw("sync", fin2)
            kb.flush(blk, scope="ssd")
            bnd_ssd = kb.boundary() + [kb.epoch("yst")]
        if stage == "ssd":
            return nc

        with contextlib.ExitStack() as p3:
            SB3 = lambda name, shape, dt: p3.enter_context(nc.sbuf_tensor(name, shape, dt))
            inter_res = SB3("inter_res", [128, 30 * 1024], BF16)
            x2_all_st = None

            with contextlib.ExitStack() as pa:
                SBa = lambda name, shape, dt: pa.enter_context(nc.sbuf_tensor(name, shape, dt))
                PSa = lambda name, shape, dt: pa.enter_context(nc.psum_tensor(name, shape, dt))
                yn_sb = SBa("yn_sb", [128, 32 * 1024], BF16)
                ynb = [SBa(f"ynb{i}", [128, 1024], BF16) for i in range(4)]
                oslab = [SBa(f"oslab{i}", [128, 4096], BF16) for i in range(3)]
                xres = [SBa(f"xres{i}", [128, 512], F32) for i in range(4)]
                x2t = [SBa(f"x2t{i}", [128, 512], F32) for i in range(4)]
                ps_o2 = [PSa(f"ps_o{i}", [128, 1024], F32) for i in range(2)]

                yn_ready = None
                for kk in range(32):
                    m, cc = kk // 8, kk % 8
                    bA, bB = ynb[(kk % 2) * 2], ynb[(kk % 2) * 2 + 1]
                    w = [kb.vtick()] if kk >= 2 else []
                    kb.dma("sync", lambda e, m=m, cc=cc, bA=bA: e.dma_start(
                        out=bA[:], in_=gout.ap()[m * 1024:(m + 1) * 1024, cc * 128:(cc + 1) * 128],
                        transpose=True), waits=w + [("co", 1)] + (bnd_ssd if kk < 2 else []),
                        stream=f"yl{kk % 2}")
                    kb.dma("sync", lambda e, m=m, cc=cc, bB=bB: e.dma_start(
                        out=bB[:],
                        in_=gout.ap()[(m + 4) * 1024:(m + 5) * 1024, cc * 128:(cc + 1) * 128],
                        transpose=True), waits=w, stream=f"yl{kk % 2}")
                    lAB = kb.epoch(f"yl{kk % 2}")
                    kb.v(lambda e, kk=kk, bA=bA: e.tensor_scalar(
                        yn_sb[:, kk * 1024:(kk + 1) * 1024], bA[:], bm0[:, 0:1], None, ALU.mult),
                        waits=[lAB, ld_bm0])
                    yn_ready = kb.v(lambda e, kk=kk, bB=bB: e.scalar_tensor_tensor(
                        yn_sb[:, kk * 1024:(kk + 1) * 1024], bB[:], bm1[:, 0:1],
                        yn_sb[:, kk * 1024:(kk + 1) * 1024], ALU.mult, ALU.add))
                oslab_done = {}
                x2st = {}
                ev_prev = {}
                for d in range(32):
                    p = d % 2
                    sb = oslab[d % 3]
                    pso = ps_o2[p]
                    w = [oslab_done[d - 3]] if d - 3 in oslab_done else []
                    sl = kb.dma("gpsimd", lambda e, d=d, sb=sb: e.dma_start(out=sb[:], in_=w_out_t[d]),
                                waits=w + (bnd_ssd if d < 3 else []), stream=f"osl{d % 3}")
                    xrs = {}
                    for tts in range(2):
                        bi = p * 2 + tts
                        w2 = [ev_prev[bi]] if bi in ev_prev else []
                        xrs[tts] = kb.dma("sync", lambda e, d=d, tts=tts, bi=bi: e.dma_start(
                            out=xres[bi][:],
                            in_=xq_fm[d * 128:(d + 1) * 128, tts * 512:(tts + 1) * 512]),
                            waits=w2 + (bnd_ssd if d == 0 else []), stream=f"xrs{bi}")
                    mmw = [sl]
                    if d == 0:
                        mmw.append(yn_ready)
                    for tts in range(2):
                        if (p * 2 + tts) in ev_prev:
                            mmw.append(ev_prev[p * 2 + tts])
                    mds = {}
                    for k in range(32):
                        for tts in range(2):
                            mds[tts] = kb.t(lambda e, k=k, tts=tts, sb=sb, pso=pso: e.matmul(
                                pso[:, tts * 512:(tts + 1) * 512],
                                sb[:, k * 128:(k + 1) * 128],
                                yn_sb[:, k * 1024 + tts * 512:k * 1024 + (tts + 1) * 512],
                                start=(k == 0), stop=(k == 31)),
                                waits=mmw if (k == 0 and tts == 0) else [],
                                inc=(k == 31))
                    for tts in range(2):
                        bi = p * 2 + tts
                        w3 = [mds[tts], xrs[tts]]
                        if bi in x2st:
                            w3.append(x2st[bi])
                        ev = kb.v(lambda e, tts=tts, bi=bi, pso=pso: e.tensor_add(
                            x2t[bi][:], pso[:, tts * 512:(tts + 1) * 512], xres[bi][:]),
                            waits=w3)
                        ev_prev[bi] = ev
                        x2st[bi] = kb.dma("sync", lambda e, d=d, tts=tts, bi=bi: e.dma_start(
                            out=d_x2[d * 128:(d + 1) * 128, tts * 512:(tts + 1) * 512],
                            in_=x2t[bi][:]), waits=[ev], stream=f"x2s{bi}")
                    oslab_done[d] = mds[1]
                x2_eps = [kb.epoch(f"x2s{i}") for i in range(4)]
                kb.flush(blk, scope="p2a")
                bnd_2a = kb.boundary() + x2_eps + [kb.epoch(f"xrs{i}") for i in range(4)]

            bc_scope = contextlib.ExitStack()
            ln2 = bc_scope.enter_context(nc.sbuf_tensor("ln2", [128, 32 * 1024], BF16))

            with contextlib.ExitStack() as pb:
                SBb = lambda name, shape, dt: pb.enter_context(nc.sbuf_tensor(name, shape, dt))
                PSb = lambda name, shape, dt: pb.enter_context(nc.psum_tensor(name, shape, dt))
                x2bf = SBb("x2bf", [128, 32 * 1024], BF16)
                sq2 = [SBb(f"sq2{i}", [128, 1024], BF16) for i in range(2)]
                m2rep = SBb("m2rep", [128, 1024], F32)
                r2rep = SBb("r2rep", [128, 1024], F32)
                t2bf = SBb("t2bf", [128, 1024], BF16)
                m2_row = m2rep[0:1, :]
                v2_row = r2rep[0:1, :]
                ps_st2 = PSb("ps_st2", [1, 2048], F32)
                ps_rep2 = PSb("ps_rep2", [128, 1024], F32)

                x2ld = [None] * 32
                for dc in range(32):
                    kb.dma("gpsimd", lambda e, dc=dc: e.dma_start(
                        out=x2bf[:, dc * 1024:(dc + 1) * 1024],
                        in_=d_x2[dc * 128:(dc + 1) * 128, :]),
                        waits=(x2_eps + bnd_2a) if dc == 0 else [], stream="x2bf")
                x2bf_done = kb.epoch("x2bf")
                for dc in range(32):
                    x2ld[dc] = x2bf_done
                sqm = [None] * 32
                for dc in range(32):
                    buf = sq2[dc % 2]
                    w = [x2ld[dc]]
                    if dc >= 2:
                        w.append(sqm[dc - 2])
                    sq = kb.s(lambda e, dc=dc, buf=buf: e.activation(
                        buf[:], x2bf[:, dc * 1024:(dc + 1) * 1024], AF.Square), waits=w)
                    for tts in range(2):
                        kb.t(lambda e, dc=dc, tts=tts: e.matmul(
                            ps_st2[0:1, tts * 512:(tts + 1) * 512], ones_colbf[:],
                            x2bf[:, dc * 1024 + tts * 512:dc * 1024 + (tts + 1) * 512],
                            start=(dc == 0), stop=(dc == 31)),
                            waits=[x2ld[dc]])
                        sqm[dc] = kb.t(lambda e, dc=dc, tts=tts, buf=buf: e.matmul(
                            ps_st2[0:1, 1024 + tts * 512:1024 + (tts + 1) * 512], ones_colbf[:],
                            buf[:, tts * 512:(tts + 1) * 512],
                            start=(dc == 0), stop=(dc == 31)),
                            waits=[sq], inc=(tts == 1))
                kb.v(lambda e: e.tensor_scalar(m2_row, ps_st2[0:1, 0:1024], 1.0 / Dm, None, ALU.mult),
                     waits=[sqm[31]])
                kb.v(lambda e: e.tensor_scalar(v2_row, ps_st2[0:1, 1024:2048], 1.0 / Dm, None, ALU.mult))
                kb.v(lambda e: e.tensor_mul(t2bf[0:1, :], m2_row, m2_row))
                v_v2 = kb.v(lambda e: e.tensor_sub(v2_row, v2_row, t2bf[0:1, :]))
                kb.s(lambda e: e.activation(v2_row, v2_row, AF.Ln, bias=kcol[0:1, 0:1]), waits=[v_v2])
                s_r2 = kb.s(lambda e: e.activation(v2_row, v2_row, AF.Exp, scale=-0.5))
                mm = None
                for tts in range(2):
                    mm = kb.t(lambda e, tts=tts: e.matmul(
                        ps_rep2[:, tts * 512:(tts + 1) * 512], ones_row[:],
                        m2_row[0:1, tts * 512:(tts + 1) * 512], start=True, stop=True),
                        waits=[kb.vtick()] if tts == 0 else [], inc=(tts == 1))
                evm = kb.v(lambda e: e.tensor_copy(m2rep[:], ps_rep2[:]), waits=[mm])
                for tts in range(2):
                    mm = kb.t(lambda e, tts=tts: e.matmul(
                        ps_rep2[:, tts * 512:(tts + 1) * 512], ones_row[:],
                        v2_row[0:1, tts * 512:(tts + 1) * 512], start=True, stop=True),
                        waits=[s_r2, evm] if tts == 0 else [], inc=(tts == 1))
                evr = kb.v(lambda e: e.tensor_copy(r2rep[:], ps_rep2[:]), waits=[mm])
                for dc in range(32):
                    kb.v(lambda e, dc=dc: e.tensor_sub(
                        t2bf[:], x2bf[:, dc * 1024:(dc + 1) * 1024], m2rep[:]),
                        waits=[evm, evr] if dc == 0 else [])
                    kb.v(lambda e, dc=dc: e.tensor_mul(
                        ln2[:, dc * 1024:(dc + 1) * 1024], t2bf[:], r2rep[:]))
                kb.flush(blk, scope="p2b")
                bnd_2b = kb.boundary()

            with contextlib.ExitStack() as pc:
                SBc = lambda name, shape, dt: pc.enter_context(nc.sbuf_tensor(name, shape, dt))
                PSc = lambda name, shape, dt: pc.enter_context(nc.psum_tensor(name, shape, dt))
                gslab = [SBc(f"gslab{i}", [128, 4096], BF16) for i in range(3)]
                uslab = [SBc(f"uslab{i}", [128, 4096], BF16) for i in range(3)]
                gs_tmp = [SBc(f"gs_tmp{i}", [128, 512], BF16) for i in range(4)]
                u_tmp = [SBc(f"u_tmp{i}", [128, 512], BF16) for i in range(4)]
                inter_st = [SBc(f"inter_st{i}", [128, 1024], BF16) for i in range(2)]
                ps_g2 = [PSc(f"ps_g{i}", [128, 1024], F32) for i in range(2)]
                ps_u2 = [PSc(f"ps_u{i}", [128, 1024], F32) for i in range(2)]

                cslab_done = {}
                inter_dma = {}
                ep_prev = {}
                for ci in range(NCI):
                    p = ci % 2
                    si = ci % 3
                    gsb, usb = gslab[si], uslab[si]
                    psg, psu = ps_g2[p], ps_u2[p]
                    w = [cslab_done[ci - 3]] if ci - 3 in cslab_done else []
                    slg = kb.dma("gpsimd", lambda e, ci=ci, gsb=gsb: e.dma_start(
                        out=gsb[:], in_=w_gate_t[ci]), waits=w + (bnd_2b if ci < 3 else []),
                        stream=f"gsl{si}")
                    slu = kb.dma("gpsimd", lambda e, ci=ci, usb=usb: e.dma_start(
                        out=usb[:], in_=w_up_t[ci]), waits=w, stream=f"usl{si}")
                    md = None
                    for k in range(32):
                        for tts in range(2):
                            w2 = []
                            if k == 0 and tts == 0:
                                w2 = [slg, slu]
                                if ci - 2 in ep_prev:
                                    w2 += ep_prev[ci - 2]
                            kb.t(lambda e, k=k, tts=tts, gsb=gsb, psg=psg: e.matmul(
                                psg[:, tts * 512:(tts + 1) * 512],
                                gsb[:, k * 128:(k + 1) * 128],
                                ln2[:, k * 1024 + tts * 512:k * 1024 + (tts + 1) * 512],
                                start=(k == 0), stop=(k == 31)),
                                waits=w2)
                        for tts in range(2):
                            md = kb.t(lambda e, k=k, tts=tts, usb=usb, psu=psu: e.matmul(
                                psu[:, tts * 512:(tts + 1) * 512],
                                usb[:, k * 128:(k + 1) * 128],
                                ln2[:, k * 1024 + tts * 512:k * 1024 + (tts + 1) * 512],
                                start=(k == 0), stop=(k == 31)),
                                waits=[],
                                inc=(k == 31 and tts == 1))
                    cslab_done[ci] = md
                    iv = None
                    gact = None
                    for tts in range(2):
                        w2 = [md, ld_gs]
                        gact = kb.s(lambda e, tts=tts, p=p, psg=psg: e.activation(
                            gs_tmp[p * 2 + tts][:], psg[:, tts * 512:(tts + 1) * 512],
                            AF.Silu, scale=gscol[:, 0:1]),
                            waits=w2)
                        uv = kb.v(lambda e, tts=tts, p=p, psu=psu: e.tensor_scalar(
                            u_tmp[p * 2 + tts][:], psu[:, tts * 512:(tts + 1) * 512],
                            uscol[:, 0:1], None, ALU.mult),
                            waits=[md, ld_us])
                        if ci < 30:
                            iv = kb.v(lambda e, tts=tts, p=p, ci=ci: e.tensor_mul(
                                inter_res[:, ci * 1024 + tts * 512:ci * 1024 + (tts + 1) * 512],
                                gs_tmp[p * 2 + tts][:], u_tmp[p * 2 + tts][:]),
                                waits=[gact])
                        else:
                            w3 = [gact]
                            if ci - 2 in inter_dma:
                                w3.append(inter_dma[ci - 2])
                            iv = kb.v(lambda e, tts=tts, p=p: e.tensor_mul(
                                inter_st[p][:, tts * 512:(tts + 1) * 512],
                                gs_tmp[p * 2 + tts][:], u_tmp[p * 2 + tts][:]),
                                waits=w3)
                    ep_prev[ci] = [gact, iv]
                    if ci >= 30:
                        inter_dma[ci] = kb.dma("sync", lambda e, ci=ci, p=p: e.dma_start(
                            out=d_inter[ci * 128:(ci + 1) * 128, :], in_=inter_st[p][:]), waits=[iv],
                            stream=f"ist{p}")
                inter_all = [kb.epoch("ist0"), kb.epoch("ist1"), kb.vtick()]
                kb.flush(blk, scope="p2c")
                bnd_2c = kb.boundary() + inter_all
            bc_scope.close()

            with contextlib.ExitStack() as pd:
                SBd = lambda name, shape, dt: pd.enter_context(nc.sbuf_tensor(name, shape, dt))
                PSd = lambda name, shape, dt: pd.enter_context(nc.psum_tensor(name, shape, dt))
                dslab = [SBd(f"dslab{i}", [128, INTER], BF16) for i in range(4)]
                band = [SBd(f"band{i}", [128, 1024], BF16) for i in range(6)]
                x2rd = [SBd(f"x2rd{i}", [128, 512], F32) for i in range(4)]
                o_tmp = [SBd(f"o_tmp{i}", [128, 512], F32) for i in range(4)]
                ps_d2 = [PSd(f"ps_d{i}", [128, 2048], F32) for i in range(2)]

                dslab_done = {}
                band_md = {}
                ost = {}
                ev_prev = {}
                for dp in range(16):
                    pp = dp % 2
                    ps_d = ps_d2[pp]
                    sls = []
                    for j in range(2):
                        sb = dslab[(dp % 2) * 2 + j]
                        w = [dslab_done[dp - 2]] if dp - 2 in dslab_done else []
                        sls.append(kb.dma("gpsimd", lambda e, dp=dp, j=j, sb=sb: e.dma_start(
                            out=sb[:], in_=w_down_t[2 * dp + j]),
                            waits=w + (bnd_2c if dp < 2 else []),
                            stream=f"dsl{(dp % 2) * 2 + j}"))
                    md = None
                    for k in range(NCI):
                        if k < 30:
                            bb = inter_res[:, k * 1024:(k + 1) * 1024]
                            bl = None
                        else:
                            bbuf = band[k % 6]
                            bb = bbuf[:]
                            w = list(inter_all)
                            prev_k = (dp, k - 6) if k >= 36 else (dp - 1, NCI - 6 + (k - 30))
                            if prev_k in band_md:
                                w.append(band_md[prev_k])
                            bl = kb.dma("sync", lambda e, k=k, bbuf=bbuf: e.dma_start(
                                out=bbuf[:], in_=d_inter[k * 128:(k + 1) * 128, :]),
                                waits=w + (bnd_2c if dp == 0 and k < 36 else []),
                                stream=f"ib{k % 6}")
                        for j in range(2):
                            sb = dslab[(dp % 2) * 2 + j]
                            for tts in range(2):
                                w2 = []
                                if bl is not None:
                                    w2.append(bl)
                                if k == 0:
                                    w2.append(sls[j])
                                    w2 += inter_all
                                    if (pp, j, tts) in ev_prev:
                                        w2.append(ev_prev[(pp, j, tts)])
                                md = kb.t(lambda e, j=j, k=k, tts=tts, sb=sb, bb=bb, ps_d=ps_d: e.matmul(
                                    ps_d[:, j * 1024 + tts * 512:j * 1024 + (tts + 1) * 512],
                                    sb[:, k * 128:(k + 1) * 128],
                                    bb[:, tts * 512:(tts + 1) * 512],
                                    start=(k == 0), stop=(k == NCI - 1)),
                                    waits=w2 if (j == 0 and tts == 0) or k == 0 else [],
                                    inc=(j == 1 and tts == 1))
                        band_md[(dp, k)] = md
                    dslab_done[dp] = md
                    for j in range(2):
                        for tts in range(2):
                            bi = j * 2 + tts
                            w = [ost[bi]] if bi in ost else []
                            xr = kb.dma("sync", lambda e, dp=dp, j=j, tts=tts, bi=bi: e.dma_start(
                                out=x2rd[bi][:],
                                in_=d_x2[(2 * dp + j) * 128:(2 * dp + j + 1) * 128,
                                         tts * 512:(tts + 1) * 512]), waits=w + x2_eps,
                                stream=f"xrd{bi}")
                            w3 = [md, xr, ld_ds]
                            if bi in ost:
                                w3.append(ost[bi])
                            ev = kb.v(lambda e, j=j, tts=tts, bi=bi, ps_d=ps_d: e.scalar_tensor_tensor(
                                o_tmp[bi][:],
                                ps_d[:, j * 1024 + tts * 512:j * 1024 + (tts + 1) * 512],
                                dscol[:, 0:1], x2rd[bi][:], ALU.mult, ALU.add),
                                waits=w3)
                            ev_prev[(pp, j, tts)] = ev
                            ost[bi] = kb.dma("sync", lambda e, dp=dp, j=j, tts=tts, bi=bi: e.dma_start(
                                out=out_fm[(2 * dp + j) * 128:(2 * dp + j + 1) * 128,
                                           tts * 512:(tts + 1) * 512],
                                in_=o_tmp[bi][:]), waits=[ev], stream=f"os{bi}")
                fin = [kb.epoch(f"os{i}") for i in range(4)]

                def final_wait(e, fin=fin):
                    for s, v in fin:
                        e.wait_ge(kb.sem[s], v)
                kb.raw("sync", final_wait)
                kb.flush(blk, scope="p2d")

    return nc


def _prep_inputs(inputs):
    x = np.asarray(inputs["L0_attn_hidden_states"])
    w_in = np.asarray(inputs["L0_attn_in_proj_w"])
    conv_w = np.asarray(inputs["L0_attn_conv_weight"])
    conv_b = np.asarray(inputs["L0_attn_conv_bias"])
    dt_bias = np.asarray(inputs["L0_attn_dt_bias"])
    A_log = np.asarray(inputs["L0_attn_A_log"])
    Dp = np.asarray(inputs["L0_attn_D"])
    nw = np.asarray(inputs["L0_attn_norm_weight"])
    w_out = np.asarray(inputs["L0_attn_out_proj_w"])
    w_g = np.asarray(inputs["L0_ffn_gate_w"])
    w_u = np.asarray(inputs["L0_ffn_up_w"])
    w_d = np.asarray(inputs["L0_ffn_down_w"])
    gs = np.asarray(inputs["L0_ffn_gate_scale"])
    us = np.asarray(inputs["L0_ffn_up_scale"])
    ds = np.asarray(inputs["L0_ffn_down_scale"])

    def tile_k_major(wt, nck, ncc_, csize):
        a = wt.reshape(nck, 128, ncc_, csize).transpose(2, 1, 0, 3)
        return np.ascontiguousarray(a.reshape(ncc_, 128, nck * csize))

    common = {}
    common["w_out_t"] = tile_k_major(np.ascontiguousarray(w_out.T), 32, 32, 128)
    common["w_gate_t"] = tile_k_major(np.ascontiguousarray(w_g.T), 32, 86, 128)
    common["w_up_t"] = tile_k_major(np.ascontiguousarray(w_u.T), 32, 86, 128)
    common["w_down_t"] = tile_k_major(np.ascontiguousarray(w_d.T), 86, 32, 128)
    common["gscol"] = np.ascontiguousarray(np.broadcast_to(np.float32(gs), (128, 1)))
    common["uscol"] = np.ascontiguousarray(np.broadcast_to(np.float32(us), (128, 1)))
    common["dscol"] = np.ascontiguousarray(np.broadcast_to(np.float32(ds), (128, 1)))
    jj, ii = np.meshgrid(np.arange(128), np.arange(128), indexing="ij")
    common["triu"] = (jj <= ii).astype(np.float32)
    common["ident"] = np.eye(128, dtype=np.float32)

    per_core = []
    for core in range(8):
        b, qq = core // 4, core % 4
        m = dict(common)
        m["x_fm"] = np.ascontiguousarray(x[b].T)
        m["xq_fm"] = np.ascontiguousarray(x[b].T[:, qq * 1024:(qq + 1) * 1024])
        m["bm0"] = np.full((128, 1), 1.0 if b == 0 else 0.0, np.float32)
        m["bm1"] = np.full((128, 1), 1.0 if b == 1 else 0.0, np.float32)
        rows = np.concatenate([
            np.arange(qq * 1024, (qq + 1) * 1024),
            4096 + np.arange(qq * 1024, (qq + 1) * 1024),
            8192 + np.arange(2 * qq * 128, (2 * qq + 2) * 128),
            9216 + np.arange(2 * qq * 128, (2 * qq + 2) * 128),
            10240 + np.arange(qq * 16, (qq + 1) * 16),
        ])
        wsT = np.ascontiguousarray(w_in[rows].T)
        m["w_in_t"] = tile_k_major(wsT[:, 0:2560], 32, 20, 128)
        m["w_in_dt"] = np.ascontiguousarray(
            wsT[:, 2560:2576].reshape(32, 128, 16).transpose(1, 0, 2).reshape(128, 512))
        conv_rows = np.concatenate([
            np.arange(qq * 1024, (qq + 1) * 1024),
            4096 + np.arange(2 * qq * 128, (2 * qq + 2) * 128),
            5120 + np.arange(2 * qq * 128, (2 * qq + 2) * 128),
        ])
        m["conv_w"] = np.ascontiguousarray(
            conv_w[conv_rows].reshape(12, 128, 4).transpose(1, 0, 2).reshape(128, 48))
        m["conv_b"] = np.ascontiguousarray(conv_b[conv_rows].reshape(12, 128).T)
        m["dtb"] = np.ascontiguousarray(dt_bias[qq * 16:(qq + 1) * 16].reshape(16, 1))
        m["alog"] = np.ascontiguousarray(A_log[qq * 16:(qq + 1) * 16].reshape(16, 1))
        m["drow"] = np.ascontiguousarray(
            np.repeat(Dp[qq * 16:(qq + 1) * 16], 64).reshape(1, 1024))
        m["nwrow"] = np.ascontiguousarray(nw[qq * 1024:(qq + 1) * 1024].reshape(1, 1024))
        per_core.append(m)
    return per_core


def kernel(**inputs):
    per_core = _prep_inputs(inputs)
    nc = build_program()
    res = run_bass_kernel_spmd(nc, per_core, core_ids=list(range(8)))
    out = np.empty((Bsz, L, Dm), dtype=np.float32)
    for core in range(8):
        b, qq = core // 4, core % 4
        out[b, qq * 1024:(qq + 1) * 1024, :] = res.results[core]["out_fm"].T
    return out


# version 10
# speedup vs baseline: 1.0239x; 1.0239x over previous
import contextlib
import numpy as np
import concourse.bass as bass
import concourse.mybir as mybir
from concourse.bass_utils import run_bass_kernel_spmd
from concourse.aot_env import allow_ici_all_to_all

F32 = mybir.dt.float32
BF16 = mybir.dt.bfloat16
AF = mybir.ActivationFunctionType
ALU = mybir.AluOpType

Bsz, L, Dm = 2, 4096, 4096
CPROJ = 2576
NCC = 21
TT, NTT = 1024, 4
Q, NCH = 128, 32
EPS = 1e-5
INTER = 11008
NCI = 86


class KB:
    ENGS = ("sync", "gpsimd", "vector", "scalar", "tensor")

    def __init__(self, nc, ctx):
        self.nc = nc
        self.ctx = ctx
        self.pend = {e: [] for e in self.ENGS}
        self.sem = {}
        for name in ("vs", "ss", "ts", "co"):
            self.sem[name] = ctx.enter_context(nc.semaphore(name))
        self.cnt = {k: 0 for k in self.sem}

    def _get_sem(self, name):
        if name not in self.sem:
            self.sem[name] = self.ctx.enter_context(self.nc.semaphore(name))
            self.cnt[name] = 0
        return self.sem[name]

    def _mkwait(self, eng, waits):
        for s, v in waits:
            if v and v > 0:
                eng.wait_ge(self.sem[s], v)

    def _serial(self, engname, key, make, waits):
        self.cnt[key] += 1
        tick = self.cnt[key]

        def fn(eng, make=make, waits=tuple(waits), tick=tick, key=key):
            self._mkwait(eng, waits)
            if tick > 1:
                eng.wait_ge(self.sem[key], tick - 1)
            make(eng).then_inc(self.sem[key], 1)

        self.pend[engname].append(fn)
        return (key, tick)

    def v(self, make, waits=()):
        return self._serial("vector", "vs", make, waits)

    def s(self, make, waits=()):
        return self._serial("scalar", "ss", make, waits)

    def t(self, make, waits=(), inc=False):
        tick = None
        if inc:
            self.cnt["ts"] += 1
            tick = self.cnt["ts"]

        def fn(eng, make=make, waits=tuple(waits), tick=tick):
            self._mkwait(eng, waits)
            ins = make(eng)
            if tick is not None:
                ins.then_inc(self.sem["ts"], 1)

        self.pend["tensor"].append(fn)
        return ("ts", tick) if inc else None

    def tmark(self):
        return self.t(lambda e: e.nop(), inc=True)

    def dma(self, q, make, waits=(), stream="d"):
        self._get_sem(stream)
        self.cnt[stream] += 16
        tick = self.cnt[stream]

        def fn(eng, make=make, waits=tuple(waits), key=stream):
            self._mkwait(eng, waits)
            make(eng).then_inc(self.sem[key], 16)

        self.pend[q].append(fn)
        return (stream, tick)

    def epoch(self, stream):
        self._get_sem(stream)
        return (stream, self.cnt[stream])

    def raw(self, engname, fn):
        self.pend[engname].append(fn)

    def vtick(self):
        return ("vs", self.cnt["vs"])

    def boundary(self):
        return [("vs", self.cnt["vs"]), ("ss", self.cnt["ss"]), ("ts", self.cnt["ts"])]

    def stick(self):
        return ("ss", self.cnt["ss"])

    def flush(self, block, scope=None):
        for name in self.ENGS:
            fns = self.pend[name]
            if not fns:
                continue
            self.pend[name] = []
            if scope is not None:
                def run(eng, fns=fns, scope=scope):
                    with self.nc.named_scope(scope):
                        for f in fns:
                            f(eng)
                getattr(block, name)(run)
            else:
                getattr(block, name)(lambda eng, fns=fns: [f(eng) for f in fns])


def build_program(debug=False, stage="full"):
    nc = bass.Bass("TRN2", target_bir_lowering=False)
    dt_io = "ExternalOutput" if debug else "Internal"

    def din(name, shape, dt=F32):
        return nc.dram_tensor(name, shape, dt, kind="ExternalInput").ap()

    x_fm = din("x_fm", [Dm, L])
    xq_fm = din("xq_fm", [Dm, 1024])
    w_in_t = din("w_in_t", [20, 128, 4096])
    w_in_dt = din("w_in_dt", [128, 512])
    conv_w_in = din("conv_w", [128, 48])
    conv_b_in = din("conv_b", [128, 12])
    dtb_in = din("dtb", [16, 1])
    alog_in = din("alog", [16, 1])
    drow_in = din("drow", [1, 1024])
    nwrow_in = din("nwrow", [1, 1024])
    w_out_t = din("w_out_t", [32, 128, 4096])
    w_gate_t = din("w_gate_t", [86, 128, 4096])
    w_up_t = din("w_up_t", [86, 128, 4096])
    w_down_t = din("w_down_t", [32, 128, 11008])
    gscol_in = din("gscol", [128, 1])
    uscol_in = din("uscol", [128, 1])
    dscol_in = din("dscol", [128, 1])
    bm0_in = din("bm0", [128, 1])
    bm1_in = din("bm1", [128, 1])
    triu_in = din("triu", [128, 128])
    ident_in = din("ident", [128, 128])

    out_fm = nc.dram_tensor("out_fm", [Dm, 1024], F32, kind="ExternalOutput").ap()

    d_gate = nc.dram_tensor("d_gate", [1024, L], BF16, kind=dt_io).ap()
    d_xbc = nc.dram_tensor("d_xbc", [1536, L], BF16, kind=dt_io).ap()
    d_s = nc.dram_tensor("d_s", [16, L], F32, kind=dt_io).ap()
    d_dt = nc.dram_tensor("d_dt", [16, L], F32, kind=dt_io).ap()
    d_sneg = nc.dram_tensor("d_sneg", [16, L], F32, kind="Internal").ap()
    d_E = nc.dram_tensor("d_E", [16, 32], F32, kind="Internal").ap()
    gin = nc.dram_tensor("gin", [2 * L, 1024], BF16, kind="Internal")
    dbg_gin = nc.dram_tensor("dbg_gin", [L, 1024], BF16, kind="ExternalOutput").ap() if debug else None
    gout = nc.dram_tensor("gout", [2 * L, 1024], BF16, kind="Internal")
    d_x2 = nc.dram_tensor("d_x2", [Dm, 1024], F32, kind=dt_io).ap()
    d_inter = nc.dram_tensor("d_inter", [INTER, 1024], BF16, kind=dt_io).ap()

    ctx = contextlib.ExitStack()
    with ctx:
        kb = KB(nc, ctx)
        blk = ctx.enter_context(nc.Block())
        SB = lambda name, shape, dt: ctx.enter_context(nc.sbuf_tensor(name, shape, dt))

        ident_f = SB("ident_f", [128, 128], F32)
        triu_f = SB("triu_f", [128, 128], F32)
        ones_row = SB("ones_row", [1, 128], F32)
        ones_colbf = SB("ones_colbf", [128, 1], BF16)
        a_col = SB("a_col", [16, 1], F32)
        dtb_col = SB("dtb_col", [16, 1], F32)
        gscol = SB("sb_gscol", [128, 1], F32)
        uscol = SB("sb_uscol", [128, 1], F32)
        dscol = SB("sb_dscol", [128, 1], F32)
        bm0 = SB("sb_bm0", [128, 1], F32)
        bm1 = SB("sb_bm1", [128, 1], F32)
        kcol = SB("kcol", [128, 2], F32)
        cw_sb = SB("cw_sb", [128, 48], F32)
        cb_sb = SB("cb_sb", [128, 12], F32)

        kb.dma("sync", lambda e: e.dma_start(out=ident_f[:], in_=ident_in), stream="cst")
        kb.dma("sync", lambda e: e.dma_start(out=triu_f[:], in_=triu_in), stream="cst")
        kb.dma("sync", lambda e: e.dma_start(out=dtb_col[:], in_=dtb_in), stream="cst")
        kb.dma("sync", lambda e: e.dma_start(out=gscol[:], in_=gscol_in), stream="cst")
        kb.dma("sync", lambda e: e.dma_start(out=uscol[:], in_=uscol_in), stream="cst")
        kb.dma("sync", lambda e: e.dma_start(out=dscol[:], in_=dscol_in), stream="cst")
        kb.dma("sync", lambda e: e.dma_start(out=bm0[:], in_=bm0_in), stream="cst")
        kb.dma("sync", lambda e: e.dma_start(out=bm1[:], in_=bm1_in), stream="cst")
        kb.dma("sync", lambda e: e.dma_start(out=a_col[:], in_=alog_in), stream="cst")
        kb.dma("sync", lambda e: e.dma_start(out=cw_sb[:], in_=conv_w_in), stream="cst")
        kb.dma("sync", lambda e: e.dma_start(out=cb_sb[:], in_=conv_b_in), stream="cst")

        ld_id = ld_tr = ld_dtb = ld_gs = ld_us = ld_ds = ld_bm0 = ld_bm1 = \
            ld_alog = ld_cw = ld_cb = kb.epoch("cst")
        v_ones = kb.v(lambda e: e.memset(ones_row[:], 1.0))
        kb.v(lambda e: e.memset(kcol[:, 0:1], EPS))
        v_kcol = kb.v(lambda e: e.memset(kcol[:, 1:2], 1.0))
        v_onesb = kb.v(lambda e: e.memset(ones_colbf[:], 1.0))
        s_a1 = kb.s(lambda e: e.activation(a_col[:], a_col[:], AF.Exp), waits=[ld_alog])
        v_a2 = kb.v(lambda e: e.tensor_scalar_mul(a_col[:], a_col[:], -1.0), waits=[s_a1])

        with contextlib.ExitStack() as p1:
            SB1 = lambda name, shape, dt: p1.enter_context(nc.sbuf_tensor(name, shape, dt))
            PS1 = lambda name, shape, dt: p1.enter_context(nc.psum_tensor(name, shape, dt))
            stg = SB1("stg", [128, 12 * 1027], BF16)
            x_bf = SB1("x_bf", [128, 32 * 1024], BF16)
            x2sq = [SB1(f"x2sq{i}", [128, 1024], BF16) for i in range(2)]
            slab = [SB1(f"slab{i}", [128, 4096], BF16) for i in range(3)]
            r1rep = SB1("r1rep", [128, 1024], F32)
            m_row = SB1("m_row", [1, 1024], F32)
            var_row = SB1("var_row", [1, 1024], F32)
            r_row = SB1("r_row", [1, 1024], F32)
            mneg_row = SB1("mneg_row", [1, 1024], F32)
            rsum_rows = SB1("rsum_rows", [1, 21 * 128], F32)
            rsc_sb = SB1("rsc_sb", [128, 1], F32)
            gate_tmp = [SB1(f"gate_tmp{i}", [128, 512], F32) for i in range(2)]
            sgate_bf = [SB1(f"sgate_bf{i}", [128, 512], BF16) for i in range(2)]
            cv_tmp = [SB1(f"cv_tmp{i}", [128, 1024], F32) for i in range(2)]
            dtt_tmp = [SB1(f"dtt_tmp{i}", [16, 512], F32) for i in range(2)]
            xbc_tmp = [SB1(f"xbc_tmp{i}", [128, 1024], BF16) for i in range(2)]
            ps_m2 = [PS1(f"ps_m{i}", [128, 1024], F32) for i in range(2)]
            ps_rcol = PS1("ps_rcol", [128, 1], F32)
            ps_rrow = PS1("ps_rrow", [1, 128], F32)

            for cc in range(12):
                kb.v(lambda e, cc=cc: e.memset(stg[:, cc * 1027:cc * 1027 + 3], 0.0))

            xld = [None] * 32
            sq_mm = [None] * 32
            slab_last = [None, None, None]
            x_consumed = None
            stat_ev_prev = None
            main_free = [None, None]
            rs_ev_prev = None
            gate_st = [None, None]
            gate_sil = [None, None]
            xbc_st = [None, None]
            cv_free = [None, None]
            dtt_st = [None, None]

            NTTrun = 1 if stage == "tt1" else NTT
            for tt in range(NTTrun):
                t0 = tt * TT
                for dc in range(32):
                    w = [x_consumed] if x_consumed else []
                    kb.dma(
                        "gpsimd",
                        lambda e, dc=dc, t0=t0: e.dma_start(
                            out=x_bf[:, dc * 1024:(dc + 1) * 1024],
                            in_=x_fm[dc * 128:(dc + 1) * 128, t0:t0 + TT]),
                        waits=w, stream="xbf")
                xbf_done = kb.epoch("xbf")
                for dc in range(32):
                    xld[dc] = xbf_done
                for dc in range(32):
                    buf = x2sq[dc % 2]
                    w = [xld[dc], v_onesb]
                    if dc >= 2:
                        w.append(sq_mm[dc - 2])
                    sq = kb.s(
                        lambda e, dc=dc, buf=buf: e.activation(
                            buf[:], x_bf[:, dc * 1024:(dc + 1) * 1024], AF.Square),
                        waits=w)
                    for tts in range(2):
                        w2 = [xld[dc]]
                        if dc == 0:
                            w2 += [v_onesb]
                            for mfp in main_free:
                                if mfp:
                                    w2.append(mfp)
                        kb.t(lambda e, dc=dc, tts=tts: e.matmul(
                            ps_m2[0][0:1, tts * 512:(tts + 1) * 512],
                            ones_colbf[:],
                            x_bf[:, dc * 1024 + tts * 512:dc * 1024 + (tts + 1) * 512],
                            start=(dc == 0), stop=(dc == 31)),
                            waits=w2)
                        sq_mm[dc] = kb.t(lambda e, dc=dc, tts=tts, buf=buf: e.matmul(
                            ps_m2[1][0:1, tts * 512:(tts + 1) * 512],
                            ones_colbf[:],
                            buf[:, tts * 512:(tts + 1) * 512],
                            start=(dc == 0), stop=(dc == 31)),
                            waits=[sq] + (w2[1:] if dc == 0 else []), inc=(tts == 1))
                kb.v(lambda e: e.tensor_scalar(m_row[:], ps_m2[0][0:1, 0:1024], 1.0 / Dm, None, ALU.mult),
                     waits=[sq_mm[31]])
                kb.v(lambda e: e.tensor_scalar(var_row[:], ps_m2[1][0:1, 0:1024], 1.0 / Dm, None, ALU.mult))
                kb.v(lambda e: e.tensor_mul(r_row[:], m_row[:], m_row[:]))
                v_var = kb.v(lambda e: e.tensor_sub(var_row[:], var_row[:], r_row[:]))
                stat_ev_prev = v_var
                kb.s(lambda e: e.activation(r_row[:], var_row[:], AF.Ln, bias=kcol[0:1, 0:1]), waits=[v_var, v_kcol])
                s_r = kb.s(lambda e: e.activation(r_row[:], r_row[:], AF.Exp, scale=-0.5))
                v_mn = kb.v(lambda e: e.tensor_scalar_mul(mneg_row[:], m_row[:], -1.0))
                rep_mm = None
                for tts in range(2):
                    w = [s_r, v_ones]
                    rep_mm = kb.t(lambda e, tts=tts: e.matmul(
                        ps_m2[0][:, tts * 512:(tts + 1) * 512],
                        ones_row[:], r_row[0:1, tts * 512:(tts + 1) * 512],
                        start=True, stop=True),
                        waits=w, inc=(tts == 1))
                rep_ev = kb.v(lambda e: e.tensor_copy(r1rep[:], ps_m2[0][:]), waits=[rep_mm])
                main_free[0] = rep_ev
                main_free[1] = stat_ev_prev

                for c in range(NCC):
                    cs = 128 if c < 20 else 16
                    sb = slab[(tt * NCC + c) % 3]
                    psm = ps_m2[c % 2]
                    sli = (tt * NCC + c) % 3
                    w = [slab_last[sli]] if slab_last[sli] else []
                    if c < 20:
                        sl = kb.dma("gpsimd",
                                    lambda e, c=c, sb=sb: e.dma_start(out=sb[:], in_=w_in_t[c]),
                                    waits=w, stream=f"slb{sli}")
                    else:
                        sl = kb.dma("gpsimd",
                                    lambda e, sb=sb: e.dma_start(out=sb[:, 0:512], in_=w_in_dt),
                                    waits=w, stream=f"slb{sli}")
                    rs_row_ev = None
                    if tt == 0:
                        rs_done = None
                        for k in range(32):
                            w2 = []
                            if k == 0:
                                w2 = [sl, v_onesb]
                                if rs_ev_prev:
                                    w2.append(rs_ev_prev)
                            rs_done = kb.t(lambda e, k=k, sb=sb, cs=cs: e.matmul(
                                ps_rcol[0:cs, 0:1],
                                sb[:, k * cs:(k + 1) * cs],
                                ones_colbf[:],
                                start=(k == 0), stop=(k == 31)),
                                waits=w2, inc=(k == 31))
                        rs_ev = kb.v(lambda e, cs=cs: e.tensor_copy(rsc_sb[0:cs, 0:1], ps_rcol[0:cs, 0:1]),
                                     waits=[rs_done])
                        rs_ev_prev = rs_ev
                        tp = kb.t(lambda e, cs=cs: e.transpose(
                            ps_rrow[0:1, 0:cs], rsc_sb[0:cs, 0:1], ident_f[0:cs, 0:cs]),
                            waits=[rs_ev, ld_id], inc=True)
                        rs_row_ev = kb.v(lambda e, c=c, cs=cs: e.tensor_copy(
                            rsum_rows[0:1, c * 128:c * 128 + cs], ps_rrow[0:1, 0:cs]), waits=[tp])
                    for k in range(32):
                        for tts in range(2):
                            w2 = []
                            if k == 0 and tts == 0:
                                w2 = [sl]
                                if main_free[c % 2]:
                                    w2.append(main_free[c % 2])
                            kb.t(lambda e, k=k, tts=tts, sb=sb, cs=cs, psm=psm: e.matmul(
                                psm[0:cs, tts * 512:(tts + 1) * 512],
                                sb[:, k * cs:(k + 1) * cs],
                                x_bf[:, k * 1024 + tts * 512:k * 1024 + (tts + 1) * 512],
                                start=(k == 0), stop=False),
                                waits=w2)
                    corr_w = [v_mn]
                    if rs_row_ev:
                        corr_w.append(rs_row_ev)
                    mdone = None
                    for tts in range(2):
                        mdone = kb.t(lambda e, c=c, cs=cs, tts=tts, psm=psm: e.matmul(
                            psm[0:cs, tts * 512:(tts + 1) * 512],
                            rsum_rows[0:1, c * 128:c * 128 + cs],
                            mneg_row[0:1, tts * 512:(tts + 1) * 512],
                            start=False, stop=True),
                            waits=corr_w if tts == 0 else [], inc=(tts == 1))
                    slab_last[sli] = mdone
                    evs = []
                    if c < 8:
                        for tts in range(2):
                            gb, sgb = gate_tmp[tts], sgate_bf[tts]
                            w2 = [mdone, rep_ev]
                            if gate_sil[tts]:
                                w2.append(gate_sil[tts])
                            gv = kb.v(lambda e, gb=gb, tts=tts, psm=psm: e.tensor_mul(
                                gb[:], psm[0:128, tts * 512:(tts + 1) * 512],
                                r1rep[:, tts * 512:(tts + 1) * 512]), waits=w2)
                            w3 = [gv]
                            if gate_st[tts]:
                                w3.append(gate_st[tts])
                            gs = kb.s(lambda e, gb=gb, sgb=sgb: e.activation(sgb[:], gb[:], AF.Silu),
                                      waits=w3)
                            gate_sil[tts] = gs
                            gate_st[tts] = kb.dma("sync",
                                                  lambda e, c=c, t0=t0, tts=tts, sgb=sgb: e.dma_start(
                                                      out=d_gate[c * 128:(c + 1) * 128,
                                                                 t0 + tts * 512:t0 + (tts + 1) * 512],
                                                      in_=sgb[:]),
                                                  waits=[gs], stream=f"gst{tts}")
                            evs.append(gv)
                    elif c < 20:
                        cc = c - 8
                        for tts in range(2):
                            evs.append(kb.v(lambda e, cc=cc, tts=tts, psm=psm: e.tensor_mul(
                                stg[:, cc * 1027 + 3 + tts * 512:cc * 1027 + 3 + (tts + 1) * 512],
                                psm[0:128, tts * 512:(tts + 1) * 512],
                                r1rep[:, tts * 512:(tts + 1) * 512]),
                                waits=[mdone, rep_ev]))
                    else:
                        for tts in range(2):
                            w4 = [mdone, rep_ev]
                            if dtt_st[tts]:
                                w4.append(dtt_st[tts])
                            dv = kb.v(lambda e, tts=tts, psm=psm: e.tensor_mul(
                                dtt_tmp[tts][:],
                                psm[0:16, tts * 512:(tts + 1) * 512],
                                r1rep[0:16, tts * 512:(tts + 1) * 512]),
                                waits=w4)
                            dtt_st[tts] = kb.dma("sync", lambda e, t0=t0, tts=tts: e.dma_start(
                                out=d_dt[0:16, t0 + tts * 512:t0 + (tts + 1) * 512],
                                in_=dtt_tmp[tts][:]), waits=[dv], stream=f"dts{tts}")
                            evs.append(dv)
                    main_free[c % 2] = evs[-1]
                for cc in range(12):
                    base = cc * 1027
                    cb = cv_tmp[cc % 2]
                    xb = xbc_tmp[cc % 2]
                    w = [ld_cw, ld_cb]
                    if cv_free[cc % 2]:
                        w.append(cv_free[cc % 2])
                    kb.v(lambda e, base=base, cb=cb, cc=cc: e.tensor_scalar(
                        cb[:], stg[:, base:base + 1024],
                        cw_sb[:, cc * 4:cc * 4 + 1], cb_sb[:, cc:cc + 1],
                        ALU.mult, ALU.add), waits=w)
                    for k in range(1, 4):
                        kb.v(lambda e, base=base, cb=cb, cc=cc, k=k: e.scalar_tensor_tensor(
                            cb[:], stg[:, base + k:base + k + 1024],
                            cw_sb[:, cc * 4 + k:cc * 4 + k + 1], cb[:],
                            ALU.mult, ALU.add))
                    cv_v = kb.vtick()
                    w3 = [cv_v]
                    if xbc_st[cc % 2]:
                        w3.append(xbc_st[cc % 2])
                    cv_free[cc % 2] = kb.s(
                        lambda e, cb=cb, xb=xb: e.activation(xb[:], cb[:], AF.Silu),
                        waits=w3)
                    xbc_st[cc % 2] = kb.dma("sync",
                                            lambda e, cc=cc, t0=t0, xb=xb: e.dma_start(
                                                out=d_xbc[cc * 128:(cc + 1) * 128, t0:t0 + TT],
                                                in_=xb[:]),
                                            waits=[cv_free[cc % 2]], stream=f"xst{cc % 2}")
                    kb.v(lambda e, base=base: e.tensor_copy(
                        stg[:, base:base + 3], stg[:, base + 1024:base + 1027]))
                x_consumed = mdone

            if stage in ("p1", "tt1"):
                eps1 = [kb.epoch(s) for s in ("gst0", "gst1", "xst0", "xst1", "dts0", "dts1")]

                def fin1(e, eps1=eps1):
                    for s, v in eps1:
                        e.wait_ge(kb.sem[s], v)
                kb.raw("sync", fin1)
            kb.flush(blk, scope="p1")
            spill_eps = [kb.epoch("gst0"), kb.epoch("gst1"),
                         kb.epoch("xst0"), kb.epoch("xst1")]
            dt_eps = [kb.epoch("dts0"), kb.epoch("dts1")]
            bnd_p1 = kb.boundary() + spill_eps + dt_eps
        if stage in ("p1", "tt1"):
            return nc

        with contextlib.ExitStack() as p2:
            SB2 = lambda name, shape, dt: p2.enter_context(nc.sbuf_tensor(name, shape, dt))
            PS2 = lambda name, shape, dt: p2.enter_context(nc.psum_tensor(name, shape, dt))
            dtp = SB2("dtp", [16, L], F32)
            eE = SB2("eE", [16, L], F32)
            wW = SB2("wW", [16, L], F32)
            drep = SB2("drep", [128, 1024], F32)
            nwrep = SB2("nwrep", [128, 1024], F32)
            Erepcols = SB2("Erepcols", [128, 512], F32)
            Etile = SB2("Etile", [16, 32], F32)
            Erow = SB2("Erow", [1, 512], F32)
            drow_sb = SB2("drow_sb", [1, 1024], F32)
            nwrow_sb = SB2("nwrow_sb", [1, 1024], F32)

            with contextlib.ExitStack() as p1d:
                sS = p1d.enter_context(nc.sbuf_tensor("sS", [16, L], F32))
                tmp16 = p1d.enter_context(nc.sbuf_tensor("tmp16", [16, L], F32))
                ps_tmp = p1d.enter_context(nc.psum_tensor("ps_tmp", [128, 1024], F32))
                kb.dma("sync", lambda e: e.dma_start(out=drow_sb[:], in_=drow_in), waits=bnd_p1, stream="cst2")
                kb.dma("sync", lambda e: e.dma_start(out=nwrow_sb[:], in_=nwrow_in), stream="cst2")
                kb.dma("sync", lambda e: e.dma_start(out=dtp[:], in_=d_dt),
                       waits=dt_eps, stream="cst2")
                ld_dr = ld_nw = ld_dt = kb.epoch("cst2")
                kb.v(lambda e: e.tensor_scalar(dtp[:], dtp[:], dtb_col[:, 0:1], None, ALU.add),
                     waits=[ld_dtb, ld_dt])
                s_abs = kb.s(lambda e: e.activation(tmp16[:], dtp[:], AF.Abs),
                             waits=[kb.vtick()])
                kb.s(lambda e: e.activation(tmp16[:], tmp16[:], AF.Exp, scale=-1.0))
                sp2 = kb.s(lambda e: e.activation(tmp16[:], tmp16[:], AF.Ln, bias=kcol[0:16, 1:2]))
                kb.v(lambda e: e.tensor_scalar(dtp[:], dtp[:], 0.0, None, ALU.max),
                     waits=[s_abs])
                kb.v(lambda e: e.tensor_add(dtp[:], dtp[:], tmp16[:]), waits=[sp2])
                kb.v(lambda e: e.tensor_scalar(sS[:], dtp[:], a_col[:, 0:1], None, ALU.mult),
                     waits=[v_a2])
                for c in range(NCH):
                    kb.v(lambda e, c=c: e.tensor_tensor_scan(
                        sS[:, c * 128:(c + 1) * 128], sS[:, c * 128:(c + 1) * 128],
                        sS[:, c * 128:(c + 1) * 128], 0.0, ALU.add, ALU.bypass))
                scan_done = kb.vtick()
                se = kb.s(lambda e: e.activation(eE[:], sS[:], AF.Exp), waits=[scan_done])
                v_sneg = kb.v(lambda e: e.tensor_scalar_mul(tmp16[:], sS[:], -1.0))
                for c in range(NCH):
                    kb.s(lambda e, c=c: e.activation(
                        wW[:, c * 128:(c + 1) * 128], sS[:, c * 128:(c + 1) * 128],
                        AF.Exp, scale=-1.0,
                        bias=sS[:, c * 128 + 127:c * 128 + 128]),
                        waits=[scan_done] if c == 0 else [])
                w_exp = kb.stick()
                v_w = kb.v(lambda e: e.tensor_mul(wW[:], wW[:], dtp[:]), waits=[w_exp])
                v_E = kb.v(lambda e: e.tensor_copy(Etile[:], eE[:, 127:L:128]), waits=[se])
                kb.dma("sync", lambda e: e.dma_start(out=d_s, in_=sS[:]), waits=[scan_done], stream="bnc")
                kb.dma("sync", lambda e: e.dma_start(out=d_sneg, in_=tmp16[:]), waits=[v_sneg], stream="bnc")
                kb.dma("sync", lambda e: e.dma_start(out=d_E, in_=Etile[:]), waits=[v_E], stream="bnc")
                st_s = st_sn = st_E = kb.epoch("bnc")
                ld_E = kb.dma("sync",
                              lambda e: e.dma_start(
                                  out=Erow[0:1, :].rearrange("p (h c) -> p h c", c=32),
                                  in_=d_E),
                              waits=[st_E], stream="cst2")
                emm = kb.t(lambda e: e.matmul(ps_tmp[:, 0:512], ones_row[:], Erow[:],
                                              start=True, stop=True),
                           waits=[ld_E, v_ones], inc=True)
                erep_ev = kb.v(lambda e: e.tensor_copy(Erepcols[:], ps_tmp[:, 0:512]), waits=[emm])
                dmm = kb.t(lambda e: e.matmul(ps_tmp[:, 0:512], ones_row[:], drow_sb[0:1, 0:512],
                                              start=True, stop=True), waits=[ld_dr, erep_ev])
                dmm = kb.t(lambda e: e.matmul(ps_tmp[:, 512:1024], ones_row[:], drow_sb[0:1, 512:1024],
                                              start=True, stop=True), inc=True)
                drep_ev = kb.v(lambda e: e.tensor_copy(drep[:], ps_tmp[:]), waits=[dmm])
                nmm = kb.t(lambda e: e.matmul(ps_tmp[:, 0:512], ones_row[:], nwrow_sb[0:1, 0:512],
                                              start=True, stop=True),
                           waits=[ld_nw, drep_ev])
                nmm = kb.t(lambda e: e.matmul(ps_tmp[:, 512:1024], ones_row[:], nwrow_sb[0:1, 512:1024],
                                              start=True, stop=True), inc=True)
                nwrep_ev = kb.v(lambda e: e.tensor_copy(nwrep[:], ps_tmp[:]), waits=[nmm])
                kb.flush(blk, scope="p1d")
                bnd_p1d = kb.boundary() + [kb.epoch("bnc"), kb.epoch("cst2")]

            S_sb = SB2("S_sb", [128, 1024], F32)
            S_tmp = SB2("S_tmp", [128, 1024], F32)
            S_bf = SB2("S_bf", [128, 1024], BF16)
            x_tm = [SB2(f"x_tm{i}", [128, 1024], BF16) for i in range(2)]
            sgT = [SB2(f"sgT{i}", [128, 1024], BF16) for i in range(2)]
            B_fm = [SB2(f"B_fm{i}", [128, 256], BF16) for i in range(2)]
            C_fm = [SB2(f"C_fm{i}", [128, 256], BF16) for i in range(2)]
            B_tm = [SB2(f"B_tm{i}", [128, 256], BF16) for i in range(2)]
            srow_t = [SB2(f"srow_t{i}", [2, 2048], F32) for i in range(2)]
            negs_t = [SB2(f"negs_t{i}", [2, 2048], F32) for i in range(2)]
            tcols = SB2("tcols", [128, 48], F32)
            dec_min = SB2("dec_min", [128, 512], F32)
            decT = SB2("decT", [128, 2048], BF16)
            scoreTm = SB2("scoreTm", [128, 256], BF16)
            MT = SB2("MT", [128, 2048], BF16)
            xt_w = SB2("xt_w", [128, 1024], BF16)
            YsbT = SB2("YsbT", [128, 1024], F32)
            dxt = SB2("dxt", [128, 1024], F32)
            Ysq = SB2("Ysq", [128, 1024], BF16)
            ssq_col = SB2("ssq_col", [128, 2], F32)
            rstd_col = SB2("rstd_col", [128, 2], F32)
            yn_t = SB2("yn_t", [128, 1024], F32)
            yn_bf = SB2("yn_bf", [128, 1024], BF16)
            ps_r = PS2("ps_r", [128, 512], F32)
            ps_aux = PS2("ps_aux", [128, 512], F32)
            ps_y = PS2("ps_y", [128, 1024], F32)
            ps_yi = PS2("ps_yi", [128, 1024], F32)
            ps_sloc = PS2("ps_sloc", [128, 1024], F32)

            for i in range(2):
                kb.v(lambda e, i=i: e.memset(srow_t[i][:], 1.0))
                kb.v(lambda e, i=i: e.memset(negs_t[i][:], 1.0))
            kb.v(lambda e: e.memset(S_sb[:], 0.0))
            rows_ready = kb.vtick()

            loads = [None] * NCH
            chunk_done = [None] * NCH

            def issue_loads(c):
                i = c % 2
                w = list(chunk_done[c - 2]) if c >= 2 and chunk_done[c - 2] else []
                st = f"sL{i}"
                kb.dma("sync", lambda e, c=c, i=i: e.dma_start(
                    out=x_tm[i][:], in_=d_xbc[0:1024, c * 128:(c + 1) * 128], transpose=True),
                    waits=w + spill_eps + (bnd_p1d if c < 2 else []), stream=st)
                kb.dma("sync", lambda e, c=c, i=i: e.dma_start(
                    out=sgT[i][:], in_=d_gate[0:1024, c * 128:(c + 1) * 128], transpose=True),
                    stream=st)
                for gl in range(2):
                    kb.dma("sync", lambda e, c=c, i=i, gl=gl: e.dma_start(
                        out=B_fm[i][:, gl * 128:(gl + 1) * 128],
                        in_=d_xbc[1024 + gl * 128:1024 + (gl + 1) * 128, c * 128:(c + 1) * 128]),
                        stream=st)
                    kb.dma("sync", lambda e, c=c, i=i, gl=gl: e.dma_start(
                        out=C_fm[i][:, gl * 128:(gl + 1) * 128],
                        in_=d_xbc[1280 + gl * 128:1280 + (gl + 1) * 128, c * 128:(c + 1) * 128]),
                        stream=st)
                    kb.dma("sync", lambda e, c=c, i=i, gl=gl: e.dma_start(
                        out=B_tm[i][:, gl * 128:(gl + 1) * 128],
                        in_=d_xbc[1024 + gl * 128:1024 + (gl + 1) * 128, c * 128:(c + 1) * 128],
                        transpose=True),
                        stream=st)
                kb.dma("sync", lambda e, c=c, i=i: e.dma_start(
                    out=srow_t[i][0:1, :].rearrange("p (h t) -> p h t", t=128),
                    in_=d_s[:, c * 128:(c + 1) * 128]),
                    waits=[st_s, rows_ready], stream=st)
                kb.dma("sync", lambda e, c=c, i=i: e.dma_start(
                    out=negs_t[i][1:2, :].rearrange("p (h t) -> p h t", t=128),
                    in_=d_sneg[:, c * 128:(c + 1) * 128]),
                    waits=[st_sn], stream=st)
                loads[c] = kb.epoch(st)

            issue_loads(0)
            issue_loads(1)
            prev_ss_end = None
            aux_free = nwrep_ev
            r_free = nwrep_ev
            y_free = nwrep_ev

            def bc(src_ap, nh, rep, hstep):
                return bass.AP(tensor=src_ap.tensor, offset=src_ap.offset,
                               ap=[[src_ap.ap[0][0], 128], [hstep, nh], [0, rep]])

            for c in range(NCH):
                i = c % 2
                ld = loads[c]
                w0 = [ld_id]
                if aux_free:
                    w0.append(aux_free)
                tp1 = kb.t(lambda e, c=c: e.transpose(
                    ps_aux[0:128, 256:272], dtp[0:16, c * 128:(c + 1) * 128],
                    ident_f[0:16, 0:16]), waits=w0, inc=True)
                ev1 = kb.v(lambda e: e.tensor_copy(tcols[:, 0:16], ps_aux[0:128, 256:272]), waits=[tp1])
                tp2 = kb.t(lambda e, c=c: e.transpose(
                    ps_aux[0:128, 272:288], eE[0:16, c * 128:(c + 1) * 128],
                    ident_f[0:16, 0:16]), waits=[ev1], inc=True)
                ev2 = kb.v(lambda e: e.tensor_copy(tcols[:, 16:32], ps_aux[0:128, 272:288]), waits=[tp2])
                tp3 = kb.t(lambda e, c=c: e.transpose(
                    ps_aux[0:128, 288:304], wW[0:16, c * 128:(c + 1) * 128],
                    ident_f[0:16, 0:16]), waits=[ev2, v_w], inc=True)
                ev3 = kb.v(lambda e: e.tensor_copy(tcols[:, 32:48], ps_aux[0:128, 288:304]), waits=[tp3])

                exp_prev = None
                vmin = None
                for r4 in range(4):
                    rmm = None
                    for hh in range(4):
                        h = r4 * 4 + hh
                        w2 = []
                        if hh == 0:
                            w2 = [ld]
                            w2.append(r_free if r4 == 0 else vmin)
                        rmm = kb.t(lambda e, h=h, hh=hh, i=i: e.matmul(
                            ps_r[:, hh * 128:(hh + 1) * 128],
                            negs_t[i][0:2, h * 128:(h + 1) * 128],
                            srow_t[i][0:2, h * 128:(h + 1) * 128],
                            start=True, stop=True),
                            waits=w2, inc=(hh == 3))
                    w2 = [rmm]
                    if exp_prev:
                        w2.append(exp_prev)
                    vmin = kb.v(lambda e: e.tensor_scalar(dec_min[:], ps_r[:], 0.0, None, ALU.min),
                                waits=w2)
                    exp_prev = kb.s(lambda e, r4=r4: e.activation(
                        decT[:, r4 * 512:(r4 + 1) * 512], dec_min[:], AF.Exp), waits=[vmin])
                dec_done = exp_prev
                r_free = vmin

                smm = None
                for gl in range(2):
                    w2 = [ld]
                    if gl == 0:
                        w2.append(ev3)
                    smm = kb.t(lambda e, gl=gl, i=i: e.matmul(
                        ps_aux[:, gl * 128:(gl + 1) * 128],
                        B_fm[i][:, gl * 128:(gl + 1) * 128],
                        C_fm[i][:, gl * 128:(gl + 1) * 128],
                        start=True, stop=True),
                        waits=w2, inc=(gl == 1))
                for gl in range(2):
                    kb.v(lambda e, gl=gl: e.tensor_mul(
                        scoreTm[:, gl * 128:(gl + 1) * 128],
                        ps_aux[:, gl * 128:(gl + 1) * 128], triu_f[:]),
                        waits=[smm, ld_tr] if gl == 0 else [])
                aux_free = kb.vtick()
                kb.v(lambda e: e.tensor_mul(
                    MT[:].rearrange("p (h i) -> p h i", i=128),
                    decT[:].rearrange("p (h i) -> p h i", i=128),
                    bc(tcols[:, 0:16], 16, 128, 1)),
                     waits=[dec_done])
                score_rep = bass.AP(tensor=scoreTm[:].tensor, offset=scoreTm[:].offset,
                                    ap=[[scoreTm[:].ap[0][0], 128], [128, 2], [0, 8], [1, 128]])
                kb.v(lambda e, score_rep=score_rep: e.tensor_mul(
                    MT[:].rearrange("p (g h i) -> p g h i", g=2, i=128),
                    MT[:].rearrange("p (g h i) -> p g h i", g=2, i=128), score_rep))
                mt_done = kb.vtick()
                kb.v(lambda e, i=i: e.tensor_mul(
                    xt_w[:].rearrange("p (h j) -> p h j", j=64),
                    x_tm[i][:].rearrange("p (h j) -> p h j", j=64),
                    bc(tcols[:, 32:48], 16, 64, 1)),
                     waits=[ld])
                xtw_done = kb.vtick()

                if c > 0:
                    sbf = kb.v(lambda e: e.tensor_copy(S_bf[:], S_sb[:]))
                    yimm = None
                    for gl in range(2):
                        yimm = kb.t(lambda e, gl=gl, i=i: e.matmul(
                            ps_yi[:, gl * 512:(gl + 1) * 512],
                            C_fm[i][:, gl * 128:(gl + 1) * 128],
                            S_bf[:, gl * 512:(gl + 1) * 512],
                            start=True, stop=True),
                            waits=[sbf] if gl == 0 else [], inc=(gl == 1))
                    w2 = [yimm]
                    if prev_ss_end:
                        w2.append(prev_ss_end)
                    kb.v(lambda e: e.tensor_mul(
                        YsbT[:].rearrange("p (h j) -> p h j", j=64),
                        ps_yi[:].rearrange("p (h j) -> p h j", j=64),
                        bc(tcols[:, 16:32], 16, 64, 1)),
                         waits=w2)
                else:
                    w2 = [prev_ss_end] if prev_ss_end else []
                    kb.v(lambda e: e.memset(YsbT[:], 0.0), waits=w2)
                yi_done = kb.vtick()

                ymm = None
                for h in range(16):
                    gl = h // 8
                    off = gl * 512 + (h - gl * 8) * 64
                    w2 = []
                    if h == 0:
                        w2 = [mt_done, ld]
                        if y_free:
                            w2.append(y_free)
                    ymm = kb.t(lambda e, h=h, off=off, i=i: e.matmul(
                        ps_y[:, off:off + 64],
                        MT[:, h * 128:(h + 1) * 128],
                        x_tm[i][:, h * 64:(h + 1) * 64],
                        start=True, stop=True),
                        waits=w2, inc=(h == 15))
                slmm = None
                for gl in range(2):
                    w2 = [ld]
                    if gl == 0:
                        w2.append(xtw_done)
                    slmm = kb.t(lambda e, gl=gl, i=i: e.matmul(
                        ps_sloc[:, gl * 512:(gl + 1) * 512],
                        B_tm[i][:, gl * 128:(gl + 1) * 128],
                        xt_w[:, gl * 512:(gl + 1) * 512],
                        start=True, stop=True),
                        waits=w2, inc=(gl == 1))
                es = Erepcols[:, c:c + 1]
                erep_b = bass.AP(tensor=es.tensor, offset=es.offset,
                                 ap=[[es.ap[0][0], 128], [32, 16], [0, 64]])
                kb.v(lambda e, erep_b=erep_b: e.tensor_mul(
                    S_tmp[:].rearrange("p (h j) -> p h j", j=64),
                    S_sb[:].rearrange("p (h j) -> p h j", j=64), erep_b),
                     waits=[slmm, erep_ev])
                kb.v(lambda e: e.tensor_add(S_sb[:], S_tmp[:], ps_sloc[:]))
                s_upd = kb.vtick()
                kb.v(lambda e: e.tensor_add(YsbT[:], YsbT[:], ps_y[:]), waits=[ymm])
                y_free = kb.vtick()
                kb.v(lambda e, i=i: e.tensor_mul(dxt[:], x_tm[i][:], drep[:]),
                     waits=[drep_ev])
                kb.v(lambda e: e.tensor_add(YsbT[:], YsbT[:], dxt[:]))
                ysq = kb.s(lambda e: e.activation(Ysq[:], YsbT[:], AF.Square),
                           waits=[kb.vtick()])
                for gl in range(2):
                    kb.v(lambda e, gl=gl: e.reduce_sum(
                        ssq_col[:, gl:gl + 1], Ysq[:, gl * 512:(gl + 1) * 512],
                        axis=mybir.AxisListType.X),
                        waits=[ysq] if gl == 0 else [])
                kb.s(lambda e: e.activation(rstd_col[:], ssq_col[:], AF.Ln,
                                            bias=kcol[:, 0:1], scale=1.0 / 512),
                     waits=[kb.vtick()])
                rs2 = kb.s(lambda e: e.activation(rstd_col[:], rstd_col[:], AF.Exp, scale=-0.5))
                kb.v(lambda e: e.tensor_mul(
                    yn_t[:].rearrange("p (g j) -> p g j", j=512),
                    YsbT[:].rearrange("p (g j) -> p g j", j=512),
                    bc(rstd_col[:, 0:2], 2, 512, 1)),
                     waits=[rs2])
                kb.v(lambda e: e.tensor_mul(yn_t[:], yn_t[:], nwrep[:]), waits=[nwrep_ev])
                w2 = [ld, kb.epoch("yst")]
                kb.v(lambda e, i=i: e.tensor_mul(yn_bf[:], yn_t[:], sgT[i][:]), waits=w2)
                yn_v = kb.vtick()
                kb.dma("sync", lambda e, c=c: e.dma_start(
                    out=gin.ap()[c * 128:(c + 1) * 128, :], in_=yn_bf[:]), waits=[yn_v], stream="yst")
                kb.dma("sync", lambda e, c=c: e.dma_start(
                    out=gin.ap()[L + c * 128:L + (c + 1) * 128, :], in_=yn_bf[:]), waits=[yn_v], stream="yst")
                if debug:
                    kb.dma("sync", lambda e, c=c: e.dma_start(
                        out=dbg_gin[c * 128:(c + 1) * 128, :], in_=yn_bf[:]), waits=[yn_v], stream="yst")
                prev_ss_end = kb.stick()
                chunk_done[c] = [kb.vtick(), ("ts", kb.cnt["ts"])]
                if c + 2 < NCH:
                    issue_loads(c + 2)

            yn_all = kb.epoch("yst")

            def coll(eng):
                eng.wait_ge(kb.sem["yst"], yn_all[1])
                with allow_ici_all_to_all():
                    eng.collective_compute(
                        "AllToAll", ALU.bypass,
                        replica_groups=[[0, 1, 2, 3, 4, 5, 6, 7]],
                        ins=[gin.ap().opt()],
                        outs=[gout.ap().opt()],
                    ).then_inc(kb.sem["co"], 1)

            if stage != "ssd":
                kb.raw("gpsimd", coll)
            else:
                eps2 = [kb.epoch("yst")]

                def fin2(e, eps2=eps2):
                    for s, v in eps2:
                        e.wait_ge(kb.sem[s], v)
                kb.raw("sync", fin2)
            kb.flush(blk, scope="ssd")
            bnd_ssd = kb.boundary() + [kb.epoch("yst")]
        if stage == "ssd":
            return nc

        with contextlib.ExitStack() as p3:
            SB3 = lambda name, shape, dt: p3.enter_context(nc.sbuf_tensor(name, shape, dt))
            inter_res = SB3("inter_res", [128, 30 * 1024], BF16)
            x2_all_st = None

            with contextlib.ExitStack() as pa:
                SBa = lambda name, shape, dt: pa.enter_context(nc.sbuf_tensor(name, shape, dt))
                PSa = lambda name, shape, dt: pa.enter_context(nc.psum_tensor(name, shape, dt))
                yn_sb = SBa("yn_sb", [128, 32 * 1024], BF16)
                ynb = [SBa(f"ynb{i}", [128, 1024], BF16) for i in range(4)]
                oslab = [SBa(f"oslab{i}", [128, 4096], BF16) for i in range(3)]
                xres = [SBa(f"xres{i}", [128, 512], F32) for i in range(4)]
                x2t = [SBa(f"x2t{i}", [128, 512], F32) for i in range(4)]
                ps_o2 = [PSa(f"ps_o{i}", [128, 1024], F32) for i in range(2)]

                yn_ready = None
                yn_ready_k = [None] * 32
                for kk in range(32):
                    m, cc = kk // 8, kk % 8
                    bA, bB = ynb[(kk % 2) * 2], ynb[(kk % 2) * 2 + 1]
                    w = [kb.vtick()] if kk >= 2 else []
                    kb.dma("sync", lambda e, m=m, cc=cc, bA=bA: e.dma_start(
                        out=bA[:], in_=gout.ap()[m * 1024:(m + 1) * 1024, cc * 128:(cc + 1) * 128],
                        transpose=True), waits=w + [("co", 1)] + (bnd_ssd if kk < 2 else []),
                        stream=f"yl{kk % 2}")
                    kb.dma("sync", lambda e, m=m, cc=cc, bB=bB: e.dma_start(
                        out=bB[:],
                        in_=gout.ap()[(m + 4) * 1024:(m + 5) * 1024, cc * 128:(cc + 1) * 128],
                        transpose=True), waits=w, stream=f"yl{kk % 2}")
                    lAB = kb.epoch(f"yl{kk % 2}")
                    kb.v(lambda e, kk=kk, bA=bA: e.tensor_scalar(
                        yn_sb[:, kk * 1024:(kk + 1) * 1024], bA[:], bm0[:, 0:1], None, ALU.mult),
                        waits=[lAB, ld_bm0])
                    yn_ready = kb.v(lambda e, kk=kk, bB=bB: e.scalar_tensor_tensor(
                        yn_sb[:, kk * 1024:(kk + 1) * 1024], bB[:], bm1[:, 0:1],
                        yn_sb[:, kk * 1024:(kk + 1) * 1024], ALU.mult, ALU.add))
                    yn_ready_k[kk] = yn_ready
                oslab_done = {}
                x2st = {}
                ev_prev = {}
                for d in range(32):
                    p = d % 2
                    sb = oslab[d % 3]
                    pso = ps_o2[p]
                    w = [oslab_done[d - 3]] if d - 3 in oslab_done else []
                    sl = kb.dma("gpsimd", lambda e, d=d, sb=sb: e.dma_start(out=sb[:], in_=w_out_t[d]),
                                waits=w + (bnd_ssd if d < 3 else []), stream=f"osl{d % 3}")
                    xrs = {}
                    for tts in range(2):
                        bi = p * 2 + tts
                        w2 = [ev_prev[bi]] if bi in ev_prev else []
                        xrs[tts] = kb.dma("sync", lambda e, d=d, tts=tts, bi=bi: e.dma_start(
                            out=xres[bi][:],
                            in_=xq_fm[d * 128:(d + 1) * 128, tts * 512:(tts + 1) * 512]),
                            waits=w2 + (bnd_ssd if d == 0 else []), stream=f"xrs{bi}")
                    mmw = [sl]
                    for tts in range(2):
                        if (p * 2 + tts) in ev_prev:
                            mmw.append(ev_prev[p * 2 + tts])
                    mds = {}
                    for k in range(32):
                        for tts in range(2):
                            w5 = []
                            if k == 0 and tts == 0:
                                w5 = list(mmw)
                            if d == 0 and tts == 0:
                                w5.append(yn_ready_k[k])
                            mds[tts] = kb.t(lambda e, k=k, tts=tts, sb=sb, pso=pso: e.matmul(
                                pso[:, tts * 512:(tts + 1) * 512],
                                sb[:, k * 128:(k + 1) * 128],
                                yn_sb[:, k * 1024 + tts * 512:k * 1024 + (tts + 1) * 512],
                                start=(k == 0), stop=(k == 31)),
                                waits=w5,
                                inc=(k == 31))
                    for tts in range(2):
                        bi = p * 2 + tts
                        w3 = [mds[tts], xrs[tts]]
                        if bi in x2st:
                            w3.append(x2st[bi])
                        ev = kb.v(lambda e, tts=tts, bi=bi, pso=pso: e.tensor_add(
                            x2t[bi][:], pso[:, tts * 512:(tts + 1) * 512], xres[bi][:]),
                            waits=w3)
                        ev_prev[bi] = ev
                        x2st[bi] = kb.dma("sync", lambda e, d=d, tts=tts, bi=bi: e.dma_start(
                            out=d_x2[d * 128:(d + 1) * 128, tts * 512:(tts + 1) * 512],
                            in_=x2t[bi][:]), waits=[ev], stream=f"x2s{bi}")
                    oslab_done[d] = mds[1]
                x2_eps = [kb.epoch(f"x2s{i}") for i in range(4)]
                kb.flush(blk, scope="p2a")
                bnd_2a = kb.boundary() + x2_eps + [kb.epoch(f"xrs{i}") for i in range(4)]

            bc_scope = contextlib.ExitStack()
            ln2 = bc_scope.enter_context(nc.sbuf_tensor("ln2", [128, 32 * 1024], BF16))

            with contextlib.ExitStack() as pb:
                SBb = lambda name, shape, dt: pb.enter_context(nc.sbuf_tensor(name, shape, dt))
                PSb = lambda name, shape, dt: pb.enter_context(nc.psum_tensor(name, shape, dt))
                x2bf = SBb("x2bf", [128, 32 * 1024], BF16)
                sq2 = [SBb(f"sq2{i}", [128, 1024], BF16) for i in range(2)]
                m2rep = SBb("m2rep", [128, 1024], F32)
                r2rep = SBb("r2rep", [128, 1024], F32)
                t2bf = SBb("t2bf", [128, 1024], BF16)
                m2_row = m2rep[0:1, :]
                v2_row = r2rep[0:1, :]
                ps_st2 = PSb("ps_st2", [1, 2048], F32)
                ps_rep2 = PSb("ps_rep2", [128, 1024], F32)

                x2ld = [None] * 32
                for dc in range(32):
                    kb.dma("gpsimd", lambda e, dc=dc: e.dma_start(
                        out=x2bf[:, dc * 1024:(dc + 1) * 1024],
                        in_=d_x2[dc * 128:(dc + 1) * 128, :]),
                        waits=(x2_eps + bnd_2a) if dc == 0 else [], stream="x2bf")
                x2bf_done = kb.epoch("x2bf")
                for dc in range(32):
                    x2ld[dc] = x2bf_done
                sqm = [None] * 32
                for dc in range(32):
                    buf = sq2[dc % 2]
                    w = [x2ld[dc]]
                    if dc >= 2:
                        w.append(sqm[dc - 2])
                    sq = kb.s(lambda e, dc=dc, buf=buf: e.activation(
                        buf[:], x2bf[:, dc * 1024:(dc + 1) * 1024], AF.Square), waits=w)
                    for tts in range(2):
                        kb.t(lambda e, dc=dc, tts=tts: e.matmul(
                            ps_st2[0:1, tts * 512:(tts + 1) * 512], ones_colbf[:],
                            x2bf[:, dc * 1024 + tts * 512:dc * 1024 + (tts + 1) * 512],
                            start=(dc == 0), stop=(dc == 31)),
                            waits=[x2ld[dc]])
                        sqm[dc] = kb.t(lambda e, dc=dc, tts=tts, buf=buf: e.matmul(
                            ps_st2[0:1, 1024 + tts * 512:1024 + (tts + 1) * 512], ones_colbf[:],
                            buf[:, tts * 512:(tts + 1) * 512],
                            start=(dc == 0), stop=(dc == 31)),
                            waits=[sq], inc=(tts == 1))
                kb.v(lambda e: e.tensor_scalar(m2_row, ps_st2[0:1, 0:1024], 1.0 / Dm, None, ALU.mult),
                     waits=[sqm[31]])
                kb.v(lambda e: e.tensor_scalar(v2_row, ps_st2[0:1, 1024:2048], 1.0 / Dm, None, ALU.mult))
                kb.v(lambda e: e.tensor_mul(t2bf[0:1, :], m2_row, m2_row))
                v_v2 = kb.v(lambda e: e.tensor_sub(v2_row, v2_row, t2bf[0:1, :]))
                kb.s(lambda e: e.activation(v2_row, v2_row, AF.Ln, bias=kcol[0:1, 0:1]), waits=[v_v2])
                s_r2 = kb.s(lambda e: e.activation(v2_row, v2_row, AF.Exp, scale=-0.5))
                mm = None
                for tts in range(2):
                    mm = kb.t(lambda e, tts=tts: e.matmul(
                        ps_rep2[:, tts * 512:(tts + 1) * 512], ones_row[:],
                        m2_row[0:1, tts * 512:(tts + 1) * 512], start=True, stop=True),
                        waits=[kb.vtick()] if tts == 0 else [], inc=(tts == 1))
                evm = kb.v(lambda e: e.tensor_copy(m2rep[:], ps_rep2[:]), waits=[mm])
                for tts in range(2):
                    mm = kb.t(lambda e, tts=tts: e.matmul(
                        ps_rep2[:, tts * 512:(tts + 1) * 512], ones_row[:],
                        v2_row[0:1, tts * 512:(tts + 1) * 512], start=True, stop=True),
                        waits=[s_r2, evm] if tts == 0 else [], inc=(tts == 1))
                evr = kb.v(lambda e: e.tensor_copy(r2rep[:], ps_rep2[:]), waits=[mm])
                for dc in range(32):
                    kb.v(lambda e, dc=dc: e.tensor_sub(
                        t2bf[:], x2bf[:, dc * 1024:(dc + 1) * 1024], m2rep[:]),
                        waits=[evm, evr] if dc == 0 else [])
                    kb.v(lambda e, dc=dc: e.tensor_mul(
                        ln2[:, dc * 1024:(dc + 1) * 1024], t2bf[:], r2rep[:]))
                kb.flush(blk, scope="p2b")
                bnd_2b = kb.boundary()

            with contextlib.ExitStack() as pc:
                SBc = lambda name, shape, dt: pc.enter_context(nc.sbuf_tensor(name, shape, dt))
                PSc = lambda name, shape, dt: pc.enter_context(nc.psum_tensor(name, shape, dt))
                gslab = [SBc(f"gslab{i}", [128, 4096], BF16) for i in range(3)]
                uslab = [SBc(f"uslab{i}", [128, 4096], BF16) for i in range(3)]
                gs_tmp = [SBc(f"gs_tmp{i}", [128, 512], BF16) for i in range(4)]
                u_tmp = [SBc(f"u_tmp{i}", [128, 512], BF16) for i in range(4)]
                inter_st = [SBc(f"inter_st{i}", [128, 1024], BF16) for i in range(2)]
                ps_g2 = [PSc(f"ps_g{i}", [128, 1024], F32) for i in range(2)]
                ps_u2 = [PSc(f"ps_u{i}", [128, 1024], F32) for i in range(2)]

                cslab_done = {}
                inter_dma = {}
                ep_prev = {}
                for ci in range(NCI):
                    p = ci % 2
                    si = ci % 3
                    gsb, usb = gslab[si], uslab[si]
                    psg, psu = ps_g2[p], ps_u2[p]
                    w = [cslab_done[ci - 3]] if ci - 3 in cslab_done else []
                    slg = kb.dma("gpsimd", lambda e, ci=ci, gsb=gsb: e.dma_start(
                        out=gsb[:], in_=w_gate_t[ci]), waits=w + (bnd_2b if ci < 3 else []),
                        stream=f"gsl{si}")
                    slu = kb.dma("gpsimd", lambda e, ci=ci, usb=usb: e.dma_start(
                        out=usb[:], in_=w_up_t[ci]), waits=w, stream=f"usl{si}")
                    md = None
                    for k in range(32):
                        for tts in range(2):
                            w2 = []
                            if k == 0 and tts == 0:
                                w2 = [slg, slu]
                                if ci - 2 in ep_prev:
                                    w2 += ep_prev[ci - 2]
                            kb.t(lambda e, k=k, tts=tts, gsb=gsb, psg=psg: e.matmul(
                                psg[:, tts * 512:(tts + 1) * 512],
                                gsb[:, k * 128:(k + 1) * 128],
                                ln2[:, k * 1024 + tts * 512:k * 1024 + (tts + 1) * 512],
                                start=(k == 0), stop=(k == 31)),
                                waits=w2)
                        for tts in range(2):
                            md = kb.t(lambda e, k=k, tts=tts, usb=usb, psu=psu: e.matmul(
                                psu[:, tts * 512:(tts + 1) * 512],
                                usb[:, k * 128:(k + 1) * 128],
                                ln2[:, k * 1024 + tts * 512:k * 1024 + (tts + 1) * 512],
                                start=(k == 0), stop=(k == 31)),
                                waits=[],
                                inc=(k == 31 and tts == 1))
                    cslab_done[ci] = md
                    iv = None
                    gact = None
                    for tts in range(2):
                        w2 = [md, ld_gs]
                        gact = kb.s(lambda e, tts=tts, p=p, psg=psg: e.activation(
                            gs_tmp[p * 2 + tts][:], psg[:, tts * 512:(tts + 1) * 512],
                            AF.Silu, scale=gscol[:, 0:1]),
                            waits=w2)
                        uv = kb.v(lambda e, tts=tts, p=p, psu=psu: e.tensor_scalar(
                            u_tmp[p * 2 + tts][:], psu[:, tts * 512:(tts + 1) * 512],
                            uscol[:, 0:1], None, ALU.mult),
                            waits=[md, ld_us])
                        if ci < 30:
                            iv = kb.v(lambda e, tts=tts, p=p, ci=ci: e.tensor_mul(
                                inter_res[:, ci * 1024 + tts * 512:ci * 1024 + (tts + 1) * 512],
                                gs_tmp[p * 2 + tts][:], u_tmp[p * 2 + tts][:]),
                                waits=[gact])
                        else:
                            w3 = [gact]
                            if ci - 2 in inter_dma:
                                w3.append(inter_dma[ci - 2])
                            iv = kb.v(lambda e, tts=tts, p=p: e.tensor_mul(
                                inter_st[p][:, tts * 512:(tts + 1) * 512],
                                gs_tmp[p * 2 + tts][:], u_tmp[p * 2 + tts][:]),
                                waits=w3)
                    ep_prev[ci] = [gact, iv]
                    if ci >= 30:
                        inter_dma[ci] = kb.dma("sync", lambda e, ci=ci, p=p: e.dma_start(
                            out=d_inter[ci * 128:(ci + 1) * 128, :], in_=inter_st[p][:]), waits=[iv],
                            stream=f"ist{p}")
                inter_all = [kb.epoch("ist0"), kb.epoch("ist1"), kb.vtick()]
                kb.flush(blk, scope="p2c")
                bnd_2c = kb.boundary() + inter_all
            bc_scope.close()

            with contextlib.ExitStack() as pd:
                SBd = lambda name, shape, dt: pd.enter_context(nc.sbuf_tensor(name, shape, dt))
                PSd = lambda name, shape, dt: pd.enter_context(nc.psum_tensor(name, shape, dt))
                dslab = [SBd(f"dslab{i}", [128, INTER], BF16) for i in range(4)]
                band = [SBd(f"band{i}", [128, 1024], BF16) for i in range(10)]
                x2rd = [SBd(f"x2rd{i}", [128, 512], F32) for i in range(4)]
                o_tmp = [SBd(f"o_tmp{i}", [128, 512], F32) for i in range(4)]
                ps_d2 = [PSd(f"ps_d{i}", [128, 2048], F32) for i in range(2)]

                dslab_done = {}
                band_md = {}
                ost = {}
                ev_prev = {}
                for dp in range(16):
                    pp = dp % 2
                    ps_d = ps_d2[pp]
                    sls = []
                    for j in range(2):
                        sb = dslab[(dp % 2) * 2 + j]
                        w = [dslab_done[dp - 2]] if dp - 2 in dslab_done else []
                        sls.append(kb.dma("gpsimd", lambda e, dp=dp, j=j, sb=sb: e.dma_start(
                            out=sb[:], in_=w_down_t[2 * dp + j]),
                            waits=w + (bnd_2c if dp < 2 else []),
                            stream=f"dsl{(dp % 2) * 2 + j}"))
                    md = None
                    for k in range(NCI):
                        if k < 30:
                            bb = inter_res[:, k * 1024:(k + 1) * 1024]
                            bl = None
                        else:
                            bbuf = band[k % 10]
                            bb = bbuf[:]
                            w = list(inter_all)
                            prev_k = (dp, k - 10) if k >= 40 else (dp - 1, NCI - 10 + (k - 30))
                            if prev_k in band_md:
                                w.append(band_md[prev_k])
                            bl = kb.dma("sync", lambda e, k=k, bbuf=bbuf: e.dma_start(
                                out=bbuf[:], in_=d_inter[k * 128:(k + 1) * 128, :]),
                                waits=w + (bnd_2c if dp == 0 and k < 40 else []),
                                stream=f"ib{k % 10}")
                        for j in range(2):
                            sb = dslab[(dp % 2) * 2 + j]
                            for tts in range(2):
                                w2 = []
                                if bl is not None:
                                    w2.append(bl)
                                if k == 0:
                                    w2.append(sls[j])
                                    w2 += inter_all
                                    if (pp, j, tts) in ev_prev:
                                        w2.append(ev_prev[(pp, j, tts)])
                                md = kb.t(lambda e, j=j, k=k, tts=tts, sb=sb, bb=bb, ps_d=ps_d: e.matmul(
                                    ps_d[:, j * 1024 + tts * 512:j * 1024 + (tts + 1) * 512],
                                    sb[:, k * 128:(k + 1) * 128],
                                    bb[:, tts * 512:(tts + 1) * 512],
                                    start=(k == 0), stop=(k == NCI - 1)),
                                    waits=w2 if (j == 0 and tts == 0) or k == 0 else [],
                                    inc=(j == 1 and tts == 1))
                        band_md[(dp, k)] = md
                    dslab_done[dp] = md
                    for j in range(2):
                        for tts in range(2):
                            bi = j * 2 + tts
                            w = [ost[bi]] if bi in ost else []
                            xr = kb.dma("sync", lambda e, dp=dp, j=j, tts=tts, bi=bi: e.dma_start(
                                out=x2rd[bi][:],
                                in_=d_x2[(2 * dp + j) * 128:(2 * dp + j + 1) * 128,
                                         tts * 512:(tts + 1) * 512]), waits=w + x2_eps,
                                stream=f"xrd{bi}")
                            w3 = [md, xr, ld_ds]
                            if bi in ost:
                                w3.append(ost[bi])
                            ev = kb.v(lambda e, j=j, tts=tts, bi=bi, ps_d=ps_d: e.scalar_tensor_tensor(
                                o_tmp[bi][:],
                                ps_d[:, j * 1024 + tts * 512:j * 1024 + (tts + 1) * 512],
                                dscol[:, 0:1], x2rd[bi][:], ALU.mult, ALU.add),
                                waits=w3)
                            ev_prev[(pp, j, tts)] = ev
                            ost[bi] = kb.dma("sync", lambda e, dp=dp, j=j, tts=tts, bi=bi: e.dma_start(
                                out=out_fm[(2 * dp + j) * 128:(2 * dp + j + 1) * 128,
                                           tts * 512:(tts + 1) * 512],
                                in_=o_tmp[bi][:]), waits=[ev], stream=f"os{bi}")
                fin = [kb.epoch(f"os{i}") for i in range(4)]

                def final_wait(e, fin=fin):
                    for s, v in fin:
                        e.wait_ge(kb.sem[s], v)
                kb.raw("sync", final_wait)
                kb.flush(blk, scope="p2d")

    return nc


def _prep_inputs(inputs):
    x = np.asarray(inputs["L0_attn_hidden_states"])
    w_in = np.asarray(inputs["L0_attn_in_proj_w"])
    conv_w = np.asarray(inputs["L0_attn_conv_weight"])
    conv_b = np.asarray(inputs["L0_attn_conv_bias"])
    dt_bias = np.asarray(inputs["L0_attn_dt_bias"])
    A_log = np.asarray(inputs["L0_attn_A_log"])
    Dp = np.asarray(inputs["L0_attn_D"])
    nw = np.asarray(inputs["L0_attn_norm_weight"])
    w_out = np.asarray(inputs["L0_attn_out_proj_w"])
    w_g = np.asarray(inputs["L0_ffn_gate_w"])
    w_u = np.asarray(inputs["L0_ffn_up_w"])
    w_d = np.asarray(inputs["L0_ffn_down_w"])
    gs = np.asarray(inputs["L0_ffn_gate_scale"])
    us = np.asarray(inputs["L0_ffn_up_scale"])
    ds = np.asarray(inputs["L0_ffn_down_scale"])

    def tile_k_major(wt, nck, ncc_, csize):
        a = wt.reshape(nck, 128, ncc_, csize).transpose(2, 1, 0, 3)
        return np.ascontiguousarray(a.reshape(ncc_, 128, nck * csize))

    common = {}
    common["w_out_t"] = tile_k_major(np.ascontiguousarray(w_out.T), 32, 32, 128)
    common["w_gate_t"] = tile_k_major(np.ascontiguousarray(w_g.T), 32, 86, 128)
    common["w_up_t"] = tile_k_major(np.ascontiguousarray(w_u.T), 32, 86, 128)
    common["w_down_t"] = tile_k_major(np.ascontiguousarray(w_d.T), 86, 32, 128)
    common["gscol"] = np.ascontiguousarray(np.broadcast_to(np.float32(gs), (128, 1)))
    common["uscol"] = np.ascontiguousarray(np.broadcast_to(np.float32(us), (128, 1)))
    common["dscol"] = np.ascontiguousarray(np.broadcast_to(np.float32(ds), (128, 1)))
    jj, ii = np.meshgrid(np.arange(128), np.arange(128), indexing="ij")
    common["triu"] = (jj <= ii).astype(np.float32)
    common["ident"] = np.eye(128, dtype=np.float32)

    per_core = []
    for core in range(8):
        b, qq = core // 4, core % 4
        m = dict(common)
        m["x_fm"] = np.ascontiguousarray(x[b].T)
        m["xq_fm"] = np.ascontiguousarray(x[b].T[:, qq * 1024:(qq + 1) * 1024])
        m["bm0"] = np.full((128, 1), 1.0 if b == 0 else 0.0, np.float32)
        m["bm1"] = np.full((128, 1), 1.0 if b == 1 else 0.0, np.float32)
        rows = np.concatenate([
            np.arange(qq * 1024, (qq + 1) * 1024),
            4096 + np.arange(qq * 1024, (qq + 1) * 1024),
            8192 + np.arange(2 * qq * 128, (2 * qq + 2) * 128),
            9216 + np.arange(2 * qq * 128, (2 * qq + 2) * 128),
            10240 + np.arange(qq * 16, (qq + 1) * 16),
        ])
        wsT = np.ascontiguousarray(w_in[rows].T)
        m["w_in_t"] = tile_k_major(wsT[:, 0:2560], 32, 20, 128)
        m["w_in_dt"] = np.ascontiguousarray(
            wsT[:, 2560:2576].reshape(32, 128, 16).transpose(1, 0, 2).reshape(128, 512))
        conv_rows = np.concatenate([
            np.arange(qq * 1024, (qq + 1) * 1024),
            4096 + np.arange(2 * qq * 128, (2 * qq + 2) * 128),
            5120 + np.arange(2 * qq * 128, (2 * qq + 2) * 128),
        ])
        m["conv_w"] = np.ascontiguousarray(
            conv_w[conv_rows].reshape(12, 128, 4).transpose(1, 0, 2).reshape(128, 48))
        m["conv_b"] = np.ascontiguousarray(conv_b[conv_rows].reshape(12, 128).T)
        m["dtb"] = np.ascontiguousarray(dt_bias[qq * 16:(qq + 1) * 16].reshape(16, 1))
        m["alog"] = np.ascontiguousarray(A_log[qq * 16:(qq + 1) * 16].reshape(16, 1))
        m["drow"] = np.ascontiguousarray(
            np.repeat(Dp[qq * 16:(qq + 1) * 16], 64).reshape(1, 1024))
        m["nwrow"] = np.ascontiguousarray(nw[qq * 1024:(qq + 1) * 1024].reshape(1, 1024))
        per_core.append(m)
    return per_core


def kernel(**inputs):
    per_core = _prep_inputs(inputs)
    nc = build_program()
    res = run_bass_kernel_spmd(nc, per_core, core_ids=list(range(8)))
    out = np.empty((Bsz, L, Dm), dtype=np.float32)
    for core in range(8):
        b, qq = core // 4, core % 4
        out[b, qq * 1024:(qq + 1) * 1024, :] = res.results[core]["out_fm"].T
    return out
